# version 03
# baseline (speedup 1.0000x reference)
import numpy as np
import contextlib
import concourse.bass as bass
import concourse.mybir as mybir
from concourse.bass_utils import run_bass_kernel_spmd

F32 = mybir.dt.float32
BF16 = mybir.dt.bfloat16
AF = mybir.ActivationFunctionType
ALU = mybir.AluOpType
AX = mybir.AxisListType

S = 4096
E = 1024
H = 16
DH = 64
NT = S // 128
NQT = 8
KC = 32
NE = 8
DFF = 4096
NCOMB = 64
EPS_LN = 1e-5
BIG = 1.0e30


def build_nc(debug=False):
    nc = bass.Bass("TRN2", target_bir_lowering=False)

    x_d = nc.dram_tensor("xin", [S, E], F32, kind="ExternalInput")
    xT_d = nc.dram_tensor("xtin", [E, S], F32, kind="ExternalInput")
    wqT_d = nc.dram_tensor("wqT", [E, E], F32, kind="ExternalInput")
    wkT_d = nc.dram_tensor("wkT", [E, E], F32, kind="ExternalInput")
    wvT_d = nc.dram_tensor("wvT", [E, E], F32, kind="ExternalInput")
    wrT_d = nc.dram_tensor("wrT", [E, NE], F32, kind="ExternalInput")
    g_d = nc.dram_tensor("g128", [128, 1], F32, kind="ExternalInput")
    eps_d = nc.dram_tensor("eps128", [128, 1], F32, kind="ExternalInput")
    w1T_d = nc.dram_tensor("w1T", [NE, E, DFF], F32, kind="ExternalInput")
    w2T_d = nc.dram_tensor("w2T", [NE, DFF, E], F32, kind="ExternalInput")
    out_d = nc.dram_tensor("out", [S, E], F32, kind="ExternalOutput")
    dbg = "ExternalOutput" if debug else "Internal"
    scr_qn = nc.dram_tensor("scr_qn", [S, E], BF16, kind=dbg)
    scr_kn = nc.dram_tensor("scr_kn", [S, E], BF16, kind=dbg)
    scr_x1f = nc.dram_tensor("scr_x1f", [S, E], F32, kind=dbg)
    scr_x1b = nc.dram_tensor("scr_x1b", [S, E], BF16, kind=dbg)
    if debug:
        dbg_gate = nc.dram_tensor("dbg_gate", [256, 128], F32, kind="ExternalOutput")
        dbg_gsc = nc.dram_tensor("dbg_gsc", [4096, 64], F32, kind="ExternalOutput")
        dbg_acc = nc.dram_tensor("dbg_acc", [256, 16 * E], BF16, kind="ExternalOutput")

    xT_v = xT_d.rearrange("(c p) n -> p c n", p=128)
    wqT_v = wqT_d.rearrange("(c p) n -> p c n", p=128)
    wkT_v = wkT_d.rearrange("(c p) n -> p c n", p=128)
    wvT_v = wvT_d.rearrange("(c p) n -> p c n", p=128)
    wrT_v = wrT_d.rearrange("(c p) n -> p c n", p=128)
    x_v = x_d.rearrange("(t p) n -> p t n", p=128)
    out_v = out_d.rearrange("(t p) n -> p t n", p=128)
    qn_v = scr_qn.rearrange("(t p) n -> p t n", p=128)
    kn_v = scr_kn.rearrange("(t p) n -> p t n", p=128)
    x1f_v = scr_x1f.rearrange("(t p) n -> p t n", p=128)
    x1b_v = scr_x1b.rearrange("(t p) n -> p t n", p=128)
    w1T_v = w1T_d.rearrange("e (c p) f -> e p c f", p=128)
    w2T_v = w2T_d.rearrange("e (c p) f -> e p c f", p=128)

    ctx = contextlib.ExitStack()
    with ctx:
        g_sb = ctx.enter_context(nc.sbuf_tensor("g_sb", [128, 1], F32))
        eps_sb = ctx.enter_context(nc.sbuf_tensor("eps_sb", [128, 1], F32))
        wr_sb = ctx.enter_context(nc.sbuf_tensor("wr_sb", [128, NE, NE], BF16))

        def sem(name):
            return ctx.enter_context(nc.semaphore(name))
        sW = sem("sW"); sMMA = sem("sMMA"); sSQ2 = sem("sSQ2"); sCL = sem("sCL")
        sRN = sem("sRN"); sVEA = sem("sVEA"); sSCR = sem("sSCR")
        sTRK = sem("sTRK"); sTRQ = sem("sTRQ"); sXRE = sem("sXRE")
        sSC = sem("sSC"); sEX = sem("sEX"); sAV = sem("sAV"); sQD = sem("sQD")
        sREC = sem("sREC"); sEP = sem("sEP"); sLN1 = sem("sLN1"); sLN2 = sem("sLN2")
        sLN3 = sem("sLN3"); sLN4 = sem("sLN4"); sLNV = sem("sLNV"); sX1W = sem("sX1W")
        sWFF = sem("sWFF"); sWDN = sem("sWDN"); sXT1 = sem("sXT1"); sX1D = sem("sX1D")
        sRT = sem("sRT"); sG1 = sem("sG1"); sG2 = sem("sG2"); sG3 = sem("sG3")
        sG4 = sem("sG4"); sGATE = sem("sGATE"); sMS = sem("sMS")
        sHMM = sem("sHMM"); sSILU = sem("sSILU"); sYMM = sem("sYMM"); sACC = sem("sACC")
        sXF = sem("sXF"); sF1 = sem("sF1"); sF2 = sem("sF2"); sF3 = sem("sF3")
        sF4 = sem("sF4"); sFV = sem("sFV"); sFC = sem("sFC"); sOUT = sem("sOUT")
        sLN1B = sem("sLN1B"); sF1B = sem("sF1B"); sG5 = sem("sG5"); sDBG = sem("sDBG"); sDBG2 = sem("sDBG2")
        sVCH = sem("sVCH"); vch_n = [0]
        sACH = sem("sACH"); ach_n = [0]

        ab = contextlib.ExitStack()
        vaug = ab.enter_context(nc.sbuf_tensor("vaug", [128, KC, H, DH + 1], BF16))
        pa = contextlib.ExitStack()
        wq_sb = pa.enter_context(nc.sbuf_tensor("wq_sb", [128, 8, E], BF16))
        wk_sb = pa.enter_context(nc.sbuf_tensor("wk_sb", [128, 8, E], BF16))
        wv_sb = pa.enter_context(nc.sbuf_tensor("wv_sb", [128, 8, E], BF16))
        xt_sb = pa.enter_context(nc.sbuf_tensor("xt_sb", [128, 2, 8, 128], BF16))
        qn_t = pa.enter_context(nc.sbuf_tensor("qn_t", [128, 2, E], BF16))
        kn_t = pa.enter_context(nc.sbuf_tensor("kn_t", [128, 2, E], BF16))
        ssq = pa.enter_context(nc.sbuf_tensor("ssq", [128, 2 * H], F32))
        ssq_cl = pa.enter_context(nc.sbuf_tensor("ssq_cl", [128, 2 * H], F32))
        rn_sb = pa.enter_context(nc.sbuf_tensor("rn_sb", [128, 2 * H], F32))
        sq_scr = pa.enter_context(nc.sbuf_tensor("sq_scr", [128, DH], F32))
        ps_qk = pa.enter_context(nc.psum_tensor("ps_qk", [128, 2048], F32))
        ps_v = pa.enter_context(nc.psum_tensor("ps_v", [128, 1024], F32))

        with nc.Block() as blk:
            @blk.gpsimd
            def _(eng):
                eng.dma_start(g_sb[:, :], g_d[:, :]).then_inc(sW, 16)
                eng.dma_start(eps_sb[:, :], eps_d[:, :]).then_inc(sW, 16)
                eng.dma_start(wq_sb[:, :, :], wqT_v[:, :, :]).then_inc(sW, 16)
                eng.dma_start(wk_sb[:, :, :], wkT_v[:, :, :]).then_inc(sW, 16)
                eng.dma_start(wv_sb[:, :, :], wvT_v[:, :, :]).then_inc(sW, 16)
                eng.dma_start(wr_sb[:, :, :], wrT_v[:, :, :]).then_inc(sW, 16)
                for t in range(NT):
                    if t >= 2:
                        eng.wait_ge(sMMA, t - 1)
                    eng.dma_start(xt_sb[:, t % 2, :, :],
                                  xT_v[:, :, t * 128:(t + 1) * 128]).then_inc(sW, 16)

            @blk.tensor
            def _(eng):
                for t in range(NT):
                    eng.wait_ge(sW, 96 + 16 * (t + 1))
                    if t >= 1:
                        eng.wait_ge(sVEA, t)
                    mm = None
                    for dc in range(8):
                        lhs = xt_sb[:, t % 2, dc, :]
                        for jj in range(2):
                            eng.matmul(ps_qk[:, jj * 512:(jj + 1) * 512], lhs,
                                       wq_sb[:, dc, jj * 512:(jj + 1) * 512],
                                       start=(dc == 0), stop=(dc == 7))
                            eng.matmul(ps_qk[:, 1024 + jj * 512:1024 + (jj + 1) * 512], lhs,
                                       wk_sb[:, dc, jj * 512:(jj + 1) * 512],
                                       start=(dc == 0), stop=(dc == 7))
                            mm = eng.matmul(ps_v[:, jj * 512:(jj + 1) * 512], lhs,
                                            wv_sb[:, dc, jj * 512:(jj + 1) * 512],
                                            start=(dc == 0), stop=(dc == 7))
                    mm.then_inc(sMMA, 1)

            @blk.scalar
            def _(eng):
                for t in range(NT):
                    eng.wait_ge(sMMA, t + 1)
                    a = None
                    for h in range(H):
                        eng.activation(sq_scr[:, :], ps_qk[:, h * DH:(h + 1) * DH],
                                       AF.Square, accum_out=ssq[:, h:h + 1])
                    for h in range(H):
                        a = eng.activation(sq_scr[:, :],
                                           ps_qk[:, 1024 + h * DH:1024 + (h + 1) * DH],
                                           AF.Square, accum_out=ssq[:, H + h:H + h + 1])
                    a.then_inc(sSQ2, 1)
                    eng.wait_ge(sCL, t + 1)
                    ach_n[0] += 1
                    eng.activation(ssq_cl[:, :], ssq_cl[:, :], AF.Ln).then_inc(sACH, 1)
                    eng.wait_ge(sACH, ach_n[0])
                    eng.activation(rn_sb[:, :], ssq_cl[:, :], AF.Exp,
                                   scale=-0.5).then_inc(sRN, 1)

            @blk.vector
            def _(eng):
                eng.memset(vaug[:, :, :, :], 1.0)
                for t in range(NT):
                    eng.wait_ge(sSQ2, t + 1)
                    eng.tensor_scalar(ssq_cl[:, :], ssq[:, :], 1e-24, None,
                                      op0=ALU.max).then_inc(sCL, 1)
                    eng.wait_ge(sRN, t + 1)
                    if t >= 2:
                        eng.wait_ge(sSCR, 32 * (t - 1))
                    for h in range(H):
                        eng.tensor_scalar(qn_t[:, t % 2, h * DH:(h + 1) * DH],
                                          ps_qk[:, h * DH:(h + 1) * DH],
                                          rn_sb[:, h:h + 1], None, op0=ALU.mult)
                    for h in range(H):
                        eng.tensor_scalar(kn_t[:, t % 2, h * DH:(h + 1) * DH],
                                          ps_qk[:, 1024 + h * DH:1024 + (h + 1) * DH],
                                          rn_sb[:, H + h:H + h + 1], None, op0=ALU.mult)
                    eng.tensor_copy(vaug[:, t, :, 0:DH],
                                    ps_v.rearrange("p (h d) -> p h d", h=H)[:, :, :]
                                    ).then_inc(sVEA, 1)

            @blk.sync
            def _(eng):
                for t in range(NT):
                    eng.wait_ge(sVEA, t + 1)
                    eng.dma_start(qn_v[:, t, :], qn_t[:, t % 2, :]).then_inc(sSCR, 16)
                    eng.dma_start(kn_v[:, t, :], kn_t[:, t % 2, :]).then_inc(sSCR, 16)

        pa.close()

        pb = contextlib.ExitStack()
        knT = pb.enter_context(nc.sbuf_tensor("knT", [128, 8, S], BF16))
        qnT = pb.enter_context(nc.sbuf_tensor("qnT", [128, 2, 8, 512], BF16))
        e_sb = pb.enter_context(nc.sbuf_tensor("e_sb", [128, 2, 1024], BF16))
        attn = pb.enter_context(nc.sbuf_tensor("attn", [128, 4, E], F32))
        xres = pb.enter_context(nc.sbuf_tensor("xres", [128, 4, E], F32))
        x1bt = pb.enter_context(nc.sbuf_tensor("x1bt", [128, 4, E], BF16))
        rec = pb.enter_context(nc.sbuf_tensor("rec", [128, 4], F32))
        lnst = pb.enter_context(nc.sbuf_tensor("lnst", [128, 16], F32))
        sq_b = pb.enter_context(nc.sbuf_tensor("sq_b", [128, E], F32))
        ps_sp = pb.enter_context(nc.psum_tensor("ps_sp", [128, 2048], F32))
        ps_o = [pb.enter_context(nc.psum_tensor(f"ps_o{i}", [128, 512], F32)) for i in range(4)]

        def bidx(qt, h, kc):
            return (qt * H + h) * KC + kc

        with nc.Block() as blk:
            @blk.sync
            def _(eng):
                eng.wait_ge(sSCR, 32 * NT)
                eng.wait_ge(sMMA, NT)
                for c in range(8):
                    eng.dma_start(knT[:, c, :], scr_kn[:, c * 128:(c + 1) * 128],
                                  transpose=True).then_inc(sTRK, 16)
                for qt in range(NQT):
                    if qt >= 2:
                        eng.wait_ge(sQD, qt - 1)
                    for c in range(8):
                        eng.dma_start(qnT[:, qt % 2, c, :],
                                      scr_qn[qt * 512:(qt + 1) * 512, c * 128:(c + 1) * 128],
                                      transpose=True).then_inc(sTRQ, 16)
                    if qt >= 1:
                        eng.wait_ge(sLNV, qt)
                    eng.dma_start(xres[:, :, :], x_v[:, qt * 4:(qt + 1) * 4, :]
                                  ).then_inc(sXRE, 16)
                    if qt >= 1:
                        eng.dma_start(x1f_v[:, (qt - 1) * 4:qt * 4, :], attn[:, :, :]
                                      ).then_inc(sX1W, 16)
                        eng.dma_start(x1b_v[:, (qt - 1) * 4:qt * 4, :], x1bt[:, :, :]
                                      ).then_inc(sX1W, 16)
                eng.wait_ge(sLNV, NQT)
                eng.dma_start(x1f_v[:, 28:32, :], attn[:, :, :]).then_inc(sX1W, 16)
                eng.dma_start(x1b_v[:, 28:32, :], x1bt[:, :, :]).then_inc(sX1W, 16)

            @blk.tensor
            def _(eng):
                eng.wait_ge(sTRK, 16 * 8)
                for qt in range(NQT):
                    eng.wait_ge(sTRQ, 128 * (qt + 1))
                    for h in range(H):
                        hh = qt * H + h
                        hc, hp = h // 2, (h % 2) * DH
                        lhs_q = qnT[hp:hp + DH, qt % 2, hc, :]

                        def av_unit(v, last):
                            vn = hh * 16 + v
                            eng.wait_ge(sEX, vn + 1)
                            if v == 0 and hh >= 1:
                                eng.wait_ge(sEP, hh)
                            mm = None
                            for j in range(2):
                                kc = 2 * v + j
                                for sub in range(4):
                                    mm = eng.matmul(
                                        ps_o[sub][:, 0:DH + 1],
                                        e_sb[:, vn % 2, j * 512 + sub * 128:j * 512 + (sub + 1) * 128],
                                        vaug[:, kc, h, :],
                                        start=(kc == 0), stop=(last and kc == KC - 1))
                            mm.then_inc(sAV, 1)

                        for u in range(16):
                            un = hh * 16 + u
                            if un >= 2:
                                eng.wait_ge(sEX, un - 1)
                            base = (u % 2) * 1024
                            for j in range(2):
                                kc = 2 * u + j
                                mm = eng.matmul(ps_sp[:, base + j * 512:base + (j + 1) * 512],
                                                knT[hp:hp + DH, hc, kc * 128:(kc + 1) * 128],
                                                lhs_q, start=True, stop=True)
                            mm.then_inc(sSC, 1)
                            if u >= 1:
                                av_unit(u - 1, last=False)
                        av_unit(15, last=True)
                    eng.nop().then_inc(sQD, 1)

            @blk.scalar
            def _(eng):
                eng.wait_ge(sW, 16)
                for qt in range(NQT):
                    for h in range(H):
                        hh = qt * H + h
                        for u in range(16):
                            un = hh * 16 + u
                            eng.wait_ge(sSC, un + 1)
                            if un >= 2:
                                eng.wait_ge(sAV, un - 1)
                            eng.activation(e_sb[:, un % 2, :],
                                           ps_sp[:, (u % 2) * 1024:(u % 2 + 1) * 1024],
                                           AF.Exp, scale=g_sb[:, 0:1]).then_inc(sEX, 1)
                        eng.wait_ge(sREC, hh + 1)
                        if h == 0 and qt >= 1:
                            eng.wait_ge(sX1W, 32 * qt)
                        a = None
                        for sub in range(4):
                            a = eng.activation(attn[:, sub, h * DH:(h + 1) * DH],
                                               ps_o[sub][:, 0:DH], AF.Copy,
                                               scale=rec[:, sub:sub + 1])
                        a.then_inc(sEP, 1)
                    eng.wait_ge(sLN2, qt + 1)
                    a = None
                    for sub in range(4):
                        a = eng.activation(sq_b[:, :], xres[:, sub, :], AF.Square,
                                           accum_out=lnst[:, 8 + sub:9 + sub])
                    a.then_inc(sLN3, 1)
                    eng.wait_ge(sLN3, qt + 1)
                    ach_n[0] += 1
                    eng.activation(lnst[:, 12:16], lnst[:, 8:12], AF.Ln,
                                   scale=1.0 / E, bias=eps_sb[:, 0:1]).then_inc(sACH, 1)
                    eng.wait_ge(sACH, ach_n[0])
                    eng.activation(lnst[:, 12:16], lnst[:, 12:16], AF.Exp,
                                   scale=-0.5).then_inc(sLN4, 1)

            @blk.vector
            def _(eng):
                for qt in range(NQT):
                    for h in range(H):
                        hh = qt * H + h
                        eng.wait_ge(sAV, 16 * (hh + 1))
                        r = None
                        for sub in range(4):
                            r = eng.reciprocal(rec[:, sub:sub + 1],
                                               ps_o[sub][:, DH:DH + 1])
                        r.then_inc(sREC, 1)
                    eng.wait_ge(sEP, H * (qt + 1))
                    eng.wait_ge(sXRE, 16 * (qt + 1))
                    if qt >= 1:
                        eng.wait_ge(sX1W, 32 * qt)
                    vch_n[0] += 1
                    eng.tensor_tensor(attn[:, :, :], attn[:, :, :], xres[:, :, :],
                                      op=ALU.add).then_inc(sVCH, 1)
                    eng.wait_ge(sVCH, vch_n[0])
                    eng.reduce_sum(lnst[:, 0:4].rearrange("p (c n) -> p c n", c=4)[:, :, :],
                                   attn[:, :, :], axis=AX.X).then_inc(sLN1, 1)
                    eng.wait_ge(sLN1, qt + 1)
                    eng.tensor_scalar(lnst[:, 4:8], lnst[:, 0:4], 1.0 / E, None,
                                      op0=ALU.mult).then_inc(sLN1B, 1)
                    eng.wait_ge(sLN1B, qt + 1)
                    t_ = None
                    for sub in range(4):
                        t_ = eng.tensor_scalar(xres[:, sub, :], attn[:, sub, :],
                                               lnst[:, 4 + sub:5 + sub], None,
                                               op0=ALU.subtract)
                    t_.then_inc(sLN2, 1)
                    eng.wait_ge(sLN4, qt + 1)
                    t_ = None
                    for sub in range(4):
                        t_ = eng.tensor_scalar(attn[:, sub, :], xres[:, sub, :],
                                               lnst[:, 12 + sub:13 + sub], None, op0=ALU.mult)
                    vch_n[0] += 1
                    t_.then_inc(sVCH, 1)
                    eng.wait_ge(sVCH, vch_n[0])
                    eng.tensor_copy(x1bt[:, :, :], attn[:, :, :]).then_inc(sLNV, 1)

        pb.close()
        ab.close()

        pc = contextlib.ExitStack()
        w1s = [pc.enter_context(nc.sbuf_tensor(f"w1s{k}", [128, 8, 1024], BF16)) for k in range(2)]
        w2s = [pc.enter_context(nc.sbuf_tensor(f"w2s{k}", [128, 8, 1024], BF16)) for k in range(2)]
        x1T = pc.enter_context(nc.sbuf_tensor("x1T", [128, 8, 2048], BF16))
        acc = pc.enter_context(nc.sbuf_tensor("acc", [128, 16, E], BF16))
        hT = pc.enter_context(nc.sbuf_tensor("hT", [128, 8, 512], BF16))
        gate = pc.enter_context(nc.sbuf_tensor("gate", [128, 16, NE], F32))
        gsc = pc.enter_context(nc.sbuf_tensor("gsc", [128, 64], F32))
        xf = pc.enter_context(nc.sbuf_tensor("xf", [128, E], F32))
        sfin = pc.enter_context(nc.sbuf_tensor("sfin", [128, E], F32))
        cent = pc.enter_context(nc.sbuf_tensor("cent", [128, E], F32))
        outb = pc.enter_context(nc.sbuf_tensor("outb", [128, E], F32))
        sq_c = pc.enter_context(nc.sbuf_tensor("sq_c", [128, E], F32))
        fst = pc.enter_context(nc.sbuf_tensor("fst", [128, 8], F32))
        ps_h = [pc.enter_context(nc.psum_tensor(f"ps_h{i}", [128, 512], F32)) for i in range(2)]
        ps_y = [pc.enter_context(nc.psum_tensor(f"ps_y{i}", [128, 512], F32)) for i in range(4)]

        def combo(i):
            return i // 32, (i % 32) // 4, (i % 32) % 4

        with nc.Block() as blk:
            @blk.gpsimd
            def _(eng):
                for i in range(NCOMB):
                    half, e, dq = combo(i)
                    if i == 0:
                        eng.wait_ge(sX1W, 256)
                    if i >= 2:
                        eng.wait_ge(sWDN, i - 1)
                    k = i % 2
                    eng.dma_start(w1s[k][:, :, :],
                                  w1T_v[e, :, :, dq * 1024:(dq + 1) * 1024]
                                  ).then_inc(sWFF, 16)
                    eng.dma_start(w2s[k][:, :, :],
                                  w2T_v[e, :, dq * 8:(dq + 1) * 8, :]
                                  ).then_inc(sWFF, 16)

            @blk.sync
            def _(eng):
                for half in range(2):
                    eng.wait_ge(sX1W, 256)
                    if half == 1:
                        eng.wait_ge(sX1D, 32)
                    for c in range(8):
                        eng.dma_start(x1T[:, c, :],
                                      scr_x1b[half * 2048:(half + 1) * 2048,
                                              c * 128:(c + 1) * 128],
                                      transpose=True).then_inc(sXT1, 16)
                    for tt in range(16):
                        g = half * 16 + tt
                        eng.wait_ge(sACC, 1024 * (half + 1))
                        if g >= 1:
                            eng.wait_ge(sFC, g)
                        eng.dma_start(xf[:, :], x1f_v[:, g, :]).then_inc(sXF, 16)
                        if tt >= 1:
                            eng.wait_ge(sFV, g)
                            eng.dma_start(out_v[:, g - 1, :], outb[:, :]
                                          ).then_inc(sOUT, 16)
                    eng.wait_ge(sFV, (half + 1) * 16)
                    eng.dma_start(out_v[:, half * 16 + 15, :], outb[:, :]
                                  ).then_inc(sOUT, 16)
                eng.wait_ge(sOUT, 16 * 32)

            @blk.tensor
            def _(eng):
                for half in range(2):
                    eng.wait_ge(sXT1, 128 * (half + 1))
                    eng.wait_ge(sW, 96)
                    if half == 1:
                        eng.wait_ge(sSILU, 1024)
                    for tt in range(16):
                        if tt >= 2:
                            eng.wait_ge(sGATE, 16 * half + tt - 1)
                        mm = None
                        for dc in range(8):
                            mm = eng.matmul(ps_h[tt % 2][:, 0:NE],
                                            x1T[:, dc, tt * 128:(tt + 1) * 128],
                                            wr_sb[:, dc, :],
                                            start=(dc == 0), stop=(dc == 7))
                        mm.then_inc(sRT, 1)
                    for i in range(half * 32, (half + 1) * 32):
                        _, e, dq = combo(i)
                        k = i % 2
                        eng.wait_ge(sWFF, 32 * (i + 1))
                        if i % 32 == 0:
                            eng.wait_ge(sGATE, 16 * (half + 1))
                        for tg in range(4):
                            gidx = i * 4 + tg
                            for c in range(8):
                                gs = gidx * 8 + c
                                if gs >= 2:
                                    eng.wait_ge(sSILU, gs - 1)
                                mm = None
                                for dc in range(8):
                                    mm = eng.matmul(ps_h[c % 2][:, :],
                                                    w1s[k][:, dc, c * 128:(c + 1) * 128],
                                                    x1T[:, dc, tg * 512:(tg + 1) * 512],
                                                    start=(dc == 0), stop=(dc == 7))
                                mm.then_inc(sHMM, 1)
                            for dh2 in range(2):
                                ga = gidx * 2 + dh2
                                if dh2 == 0:
                                    eng.wait_ge(sSILU, 8 * (gidx + 1))
                                mm = None
                                for sub in range(4):
                                    if ga >= 1:
                                        eng.wait_ge(sACC, 4 * (ga - 1) + sub + 1)
                                    for c in range(8):
                                        mm = eng.matmul(ps_y[sub][:, :],
                                                        hT[:, c, sub * 128:(sub + 1) * 128],
                                                        w2s[k][:, c, dh2 * 512:(dh2 + 1) * 512],
                                                        start=(c == 0), stop=(c == 7))
                                mm.then_inc(sYMM, 1)
                        eng.nop().then_inc(sWDN, 1)
                        eng.nop().then_inc(sX1D, 1)

            @blk.scalar
            def _(eng):
                for half in range(2):
                    if debug:
                        if half == 1:
                            eng.wait_ge(sACC, 1024)
                            eng.dma_start(dbg_acc[0:128, :],
                                          acc.rearrange("p a b -> p (a b)")[:, :]
                                          ).then_inc(sDBG, 16)
                    for tt in range(16):
                        g = 16 * half + tt
                        eng.wait_ge(sG2, g + 1)
                        eng.activation(gsc[:, 25:26], gsc[:, 24:25], AF.Exp
                                       ).then_inc(sG3, 1)
                        if debug:
                            eng.wait_ge(sGATE, g + 1)
                            eng.dma_start(dbg_gsc[g * 128:(g + 1) * 128, :],
                                          gsc[:, :]).then_inc(sDBG2, 16)
                    for i in range(half * 32, (half + 1) * 32):
                        for tg in range(4):
                            gidx = i * 4 + tg
                            for c in range(8):
                                gs = gidx * 8 + c
                                eng.wait_ge(sHMM, gs + 1)
                                if c == 0 and gidx >= 1:
                                    eng.wait_ge(sYMM, 2 * gidx)
                                eng.activation(hT[:, c, :], ps_h[c % 2][:, :],
                                               AF.Silu).then_inc(sSILU, 1)
                    if debug:
                        eng.wait_ge(sGATE, 16 * (half + 1))
                        eng.dma_start(dbg_gate[half * 128:(half + 1) * 128, :],
                                      gate.rearrange("p a b -> p (a b)")[:, :]
                                      ).then_inc(sDBG, 16)
                        if half == 1:
                            eng.wait_ge(sACC, 2048)
                            eng.dma_start(dbg_acc[128:256, :],
                                          acc.rearrange("p a b -> p (a b)")[:, :]
                                          ).then_inc(sDBG, 16)
                    for tt in range(16):
                        g = half * 16 + tt
                        eng.wait_ge(sF2, g + 1)
                        eng.activation(sq_c[:, :], cent[:, :], AF.Square,
                                       accum_out=fst[:, 0:1]).then_inc(sF3, 1)
                        eng.wait_ge(sF3, g + 1)
                        ach_n[0] += 1
                        eng.activation(fst[:, 1:2], fst[:, 0:1], AF.Ln,
                                       scale=1.0 / E, bias=eps_sb[:, 0:1]).then_inc(sACH, 1)
                        eng.wait_ge(sACH, ach_n[0])
                        eng.activation(fst[:, 2:3], fst[:, 1:2], AF.Exp,
                                       scale=-0.5).then_inc(sF4, 1)

            @blk.vector
            def _(eng):
                for half in range(2):
                    for tt in range(16):
                        g = 16 * half + tt
                        eng.wait_ge(sRT, g + 1)
                        if g >= 1:
                            eng.wait_ge(sG3, g)
                        if debug:
                            if g >= 1:
                                eng.wait_ge(sDBG2, 16 * g)
                            eng.tensor_copy(gsc[:, 56:64], ps_h[tt % 2][:, 0:NE])
                        eng.reduce_max(gsc[:, 16:17], ps_h[tt % 2][:, 0:NE],
                                       axis=AX.X).then_inc(sG1, 1)
                        eng.wait_ge(sG1, g + 1)
                        def VC(instr):
                            vch_n[0] += 1
                            instr.then_inc(sVCH, 1)
                            eng.wait_ge(sVCH, vch_n[0])
                        VC(eng.tensor_scalar(gsc[:, 0:NE], ps_h[tt % 2][:, 0:NE],
                                             gsc[:, 16:17], None, op0=ALU.subtract))
                        VC(eng.tensor_scalar(gsc[:, 8:8 + NE], gsc[:, 0:NE], 0.0, None,
                                             op0=ALU.is_equal))
                        VC(eng.scalar_tensor_tensor(gsc[:, 32:32 + NE], gsc[:, 8:8 + NE],
                                                    -BIG, gsc[:, 0:NE],
                                                    op0=ALU.mult, op1=ALU.add))
                        eng.reduce_max(gsc[:, 24:25], gsc[:, 32:32 + NE],
                                       axis=AX.X).then_inc(sG2, 1)
                        eng.wait_ge(sG3, g + 1)
                        VC(eng.tensor_scalar(gsc[:, 26:27], gsc[:, 25:26], 1.0, None,
                                             op0=ALU.add))
                        eng.reciprocal(gsc[:, 27:28], gsc[:, 26:27]).then_inc(sG4, 1)
                        eng.wait_ge(sG4, g + 1)
                        VC(eng.tensor_tensor(gsc[:, 28:29], gsc[:, 25:26], gsc[:, 27:28],
                                             op=ALU.mult))
                        VC(eng.tensor_scalar(gsc[:, 40:40 + NE], gsc[:, 32:32 + NE],
                                             gsc[:, 24:25], None, op0=ALU.is_equal))
                        VC(eng.tensor_scalar(gsc[:, 48:48 + NE], gsc[:, 8:8 + NE],
                                             gsc[:, 27:28], None, op0=ALU.mult))
                        eng.scalar_tensor_tensor(gate[:, tt, :], gsc[:, 40:40 + NE],
                                                 gsc[:, 28:29], gsc[:, 48:48 + NE],
                                                 op0=ALU.mult, op1=ALU.add
                                                 ).then_inc(sGATE, 1)
                    if debug and half == 1:
                        eng.wait_ge(sDBG, 32)
                    eng.memset(acc[:, :, :], 0.0).then_inc(sMS, 1)
                    for i in range(half * 32, (half + 1) * 32):
                        _, e, dq = combo(i)
                        for tg in range(4):
                            gidx = i * 4 + tg
                            for dh2 in range(2):
                                ga = gidx * 2 + dh2
                                eng.wait_ge(sYMM, ga + 1)
                                if i % 32 == 0 and tg == 0 and dh2 == 0:
                                    eng.wait_ge(sMS, half + 1)
                                for sub in range(4):
                                    tt = tg * 4 + sub
                                    eng.scalar_tensor_tensor(
                                        acc[:, tt, dh2 * 512:(dh2 + 1) * 512],
                                        ps_y[sub][:, :],
                                        gate[:, tt, e:e + 1],
                                        acc[:, tt, dh2 * 512:(dh2 + 1) * 512],
                                        op0=ALU.mult, op1=ALU.add).then_inc(sACC, 1)
                    for tt in range(16):
                        g = half * 16 + tt
                        eng.wait_ge(sXF, 16 * (g + 1))
                        vch_n[0] += 1
                        eng.tensor_tensor(sfin[:, :], xf[:, :], acc[:, tt, :],
                                          op=ALU.add).then_inc(sVCH, 1)
                        eng.wait_ge(sVCH, vch_n[0])
                        eng.nop().then_inc(sFC, 1)
                        eng.reduce_sum(fst[:, 4:5], sfin[:, :], axis=AX.X
                                       ).then_inc(sF1, 1)
                        eng.wait_ge(sF1, g + 1)
                        eng.tensor_scalar(fst[:, 5:6], fst[:, 4:5], 1.0 / E, None,
                                          op0=ALU.mult).then_inc(sF1B, 1)
                        eng.wait_ge(sF1B, g + 1)
                        eng.tensor_scalar(cent[:, :], sfin[:, :], fst[:, 5:6], None,
                                          op0=ALU.subtract).then_inc(sF2, 1)
                        eng.wait_ge(sF4, g + 1)
                        if g >= 1:
                            eng.wait_ge(sOUT, 16 * g)
                        eng.tensor_scalar(outb[:, :], cent[:, :], fst[:, 2:3], None,
                                          op0=ALU.mult).then_inc(sFV, 1)

        pc.close()
    return nc


def kernel(L0_attn_x, L0_attn_Wq, L0_attn_Wk, L0_attn_Wv, L0_attn_g_scale,
           L0_ffn_W1, L0_ffn_W2, L0_ffn_Wr):
    x = np.ascontiguousarray(np.asarray(L0_attn_x, dtype=np.float32))
    wq = np.asarray(L0_attn_Wq, dtype=np.float32)
    wk = np.asarray(L0_attn_Wk, dtype=np.float32)
    wv = np.asarray(L0_attn_Wv, dtype=np.float32)
    wr = np.asarray(L0_ffn_Wr, dtype=np.float32)
    w1 = np.asarray(L0_ffn_W1, dtype=np.float32)
    w2 = np.asarray(L0_ffn_W2, dtype=np.float32)
    g = np.asarray(L0_attn_g_scale, dtype=np.float32)

    wqT = np.ascontiguousarray(wq.T)
    wkT = np.ascontiguousarray(wk.T)
    wvT = np.ascontiguousarray(wv.T)
    wrT = np.ascontiguousarray(wr.T)
    w1T = np.ascontiguousarray(w1.transpose(0, 2, 1))
    w2T = np.ascontiguousarray(w2.transpose(0, 2, 1))
    g128 = np.ascontiguousarray(np.broadcast_to(g.reshape(1, 1), (128, 1)))
    eps128 = np.full((128, 1), 1e-5, dtype=np.float32)

    nc = build_nc()
    in_maps = []
    for c in range(8):
        in_maps.append({
            "xin": np.ascontiguousarray(x[c]),
            "xtin": np.ascontiguousarray(x[c].T),
            "wqT": wqT, "wkT": wkT, "wvT": wvT, "wrT": wrT,
            "g128": g128, "eps128": eps128, "w1T": w1T, "w2T": w2T,
        })
    res = run_bass_kernel_spmd(nc, in_maps, core_ids=list(range(8)))
    return np.stack([r["out"] for r in res.results], axis=0)


# version 07
# speedup vs baseline: 1.0409x; 1.0409x over previous
import numpy as np
import contextlib
import concourse.bass as bass
import concourse.mybir as mybir
from concourse.bass_utils import run_bass_kernel_spmd

F32 = mybir.dt.float32
BF16 = mybir.dt.bfloat16
AF = mybir.ActivationFunctionType
ALU = mybir.AluOpType
AX = mybir.AxisListType

S = 4096
E = 1024
H = 16
DH = 64
NT = S // 128
NQT = 8
KC = 32
NE = 8
DFF = 4096
NCOMB = 64
EPS_LN = 1e-5
BIG = 1.0e30


def build_nc(debug=False):
    nc = bass.Bass("TRN2", target_bir_lowering=False)

    x_d = nc.dram_tensor("xin", [S, E], F32, kind="ExternalInput")
    xT_d = nc.dram_tensor("xtin", [E, S], F32, kind="ExternalInput")
    wqT_d = nc.dram_tensor("wqT", [E, E], F32, kind="ExternalInput")
    wkT_d = nc.dram_tensor("wkT", [E, E], F32, kind="ExternalInput")
    wvT_d = nc.dram_tensor("wvT", [E, E], F32, kind="ExternalInput")
    wrT_d = nc.dram_tensor("wrT", [E, NE], F32, kind="ExternalInput")
    g_d = nc.dram_tensor("g128", [128, 1], F32, kind="ExternalInput")
    eps_d = nc.dram_tensor("eps128", [128, 2], F32, kind="ExternalInput")
    w1T_d = nc.dram_tensor("w1T", [NE, E, DFF], F32, kind="ExternalInput")
    w2T_d = nc.dram_tensor("w2T", [NE, DFF, E], F32, kind="ExternalInput")
    out_d = nc.dram_tensor("out", [S, E], F32, kind="ExternalOutput")
    dbg = "ExternalOutput" if debug else "Internal"
    scr_qn = nc.dram_tensor("scr_qn", [S, E], BF16, kind=dbg)
    scr_kn = nc.dram_tensor("scr_kn", [S, E], BF16, kind=dbg)
    scr_x1f = nc.dram_tensor("scr_x1f", [S, E], F32, kind=dbg)
    scr_x1b = nc.dram_tensor("scr_x1b", [S, E], BF16, kind=dbg)
    if debug:
        dbg_gate = nc.dram_tensor("dbg_gate", [256, 128], F32, kind="ExternalOutput")
        dbg_gsc = nc.dram_tensor("dbg_gsc", [4096, 64], F32, kind="ExternalOutput")
        dbg_acc = nc.dram_tensor("dbg_acc", [256, 16 * E], BF16, kind="ExternalOutput")

    xT_v = xT_d.rearrange("(c p) n -> p c n", p=128)
    wqT_v = wqT_d.rearrange("(c p) n -> p c n", p=128)
    wkT_v = wkT_d.rearrange("(c p) n -> p c n", p=128)
    wvT_v = wvT_d.rearrange("(c p) n -> p c n", p=128)
    wrT_v = wrT_d.rearrange("(c p) n -> p c n", p=128)
    x_v = x_d.rearrange("(t p) n -> p t n", p=128)
    out_v = out_d.rearrange("(t p) n -> p t n", p=128)
    qn_v = scr_qn.rearrange("(t p) n -> p t n", p=128)
    kn_v = scr_kn.rearrange("(t p) n -> p t n", p=128)
    x1f_v = scr_x1f.rearrange("(t p) n -> p t n", p=128)
    x1b_v = scr_x1b.rearrange("(t p) n -> p t n", p=128)
    w1T_v = w1T_d.rearrange("e (c p) f -> e p c f", p=128)
    w2T_v = w2T_d.rearrange("e (c p) f -> e p c f", p=128)

    ctx = contextlib.ExitStack()
    with ctx:
        g_sb = ctx.enter_context(nc.sbuf_tensor("g_sb", [128, 1], F32))
        eps_sb = ctx.enter_context(nc.sbuf_tensor("eps_sb", [128, 2], F32))
        wr_sb = ctx.enter_context(nc.sbuf_tensor("wr_sb", [128, NE, NE], BF16))

        def sem(name):
            return ctx.enter_context(nc.semaphore(name))
        sW = sem("sW"); sMMA = sem("sMMA"); sSQ2 = sem("sSQ2"); sCL = sem("sCL")
        sRN = sem("sRN"); sVEA = sem("sVEA"); sSCR = sem("sSCR")
        sTRK = sem("sTRK"); sTRQ = sem("sTRQ"); sXRE = sem("sXRE")
        sSC = sem("sSC"); sEX = sem("sEX"); sAV = sem("sAV"); sQD = sem("sQD")
        sREC = sem("sREC"); sEP = sem("sEP"); sLN1 = sem("sLN1"); sLN2 = sem("sLN2")
        sLN3 = sem("sLN3"); sLN4 = sem("sLN4"); sLNV = sem("sLNV"); sX1W = sem("sX1W")
        sWFF = sem("sWFF"); sWDN = sem("sWDN"); sXT1 = sem("sXT1"); sX1D = sem("sX1D")
        sRT = sem("sRT"); sG1 = sem("sG1"); sG2 = sem("sG2"); sG3 = sem("sG3")
        sG4 = sem("sG4"); sGATE = sem("sGATE"); sMS = sem("sMS")
        sHMM = sem("sHMM"); sSILU = sem("sSILU"); sYMM = sem("sYMM"); sACC = sem("sACC")
        sXF = sem("sXF"); sF1 = sem("sF1"); sF2 = sem("sF2"); sF3 = sem("sF3")
        sF4 = sem("sF4"); sFV = sem("sFV"); sFC = sem("sFC"); sOUT = sem("sOUT")
        sLN1B = sem("sLN1B"); sF1B = sem("sF1B"); sG5 = sem("sG5"); sDBG = sem("sDBG"); sDBG2 = sem("sDBG2")
        sVCH = sem("sVCH"); vch_n = [0]
        sACH = sem("sACH"); ach_n = [0]

        x1T0 = ctx.enter_context(nc.sbuf_tensor("x1T0", [128, 8, 2048], BF16))
        ab = contextlib.ExitStack()
        vaug = ab.enter_context(nc.sbuf_tensor("vaug", [128, KC, H, DH + 1], BF16))
        pa = contextlib.ExitStack()
        wq_sb = pa.enter_context(nc.sbuf_tensor("wq_sb", [128, 8, E], BF16))
        wk_sb = pa.enter_context(nc.sbuf_tensor("wk_sb", [128, 8, E], BF16))
        wv_sb = pa.enter_context(nc.sbuf_tensor("wv_sb", [128, 8, E], BF16))
        xt_sb = pa.enter_context(nc.sbuf_tensor("xt_sb", [128, 2, 8, 128], BF16))
        qn_t = pa.enter_context(nc.sbuf_tensor("qn_t", [128, 2, E], BF16))
        kn_t = pa.enter_context(nc.sbuf_tensor("kn_t", [128, 2, E], BF16))
        ssq = pa.enter_context(nc.sbuf_tensor("ssq", [128, 2 * H], F32))
        rn_sb = pa.enter_context(nc.sbuf_tensor("rn_sb", [128, 2 * H], F32))
        sq_scr = pa.enter_context(nc.sbuf_tensor("sq_scr", [128, DH], F32))
        ps_qk = pa.enter_context(nc.psum_tensor("ps_qk", [128, 2048], F32))
        ps_v = pa.enter_context(nc.psum_tensor("ps_v", [128, 1024], F32))

        with nc.Block() as blk:
            @blk.gpsimd
            def _(eng):
                eng.dma_start(g_sb[:, :], g_d[:, :]).then_inc(sW, 16)
                eng.dma_start(eps_sb[:, :], eps_d[:, :]).then_inc(sW, 16)
                eng.dma_start(wq_sb[:, :, :], wqT_v[:, :, :]).then_inc(sW, 16)
                eng.dma_start(wk_sb[:, :, :], wkT_v[:, :, :]).then_inc(sW, 16)
                eng.dma_start(wv_sb[:, :, :], wvT_v[:, :, :]).then_inc(sW, 16)
                eng.dma_start(wr_sb[:, :, :], wrT_v[:, :, :]).then_inc(sW, 16)
                for t in range(NT):
                    if t >= 2:
                        eng.wait_ge(sMMA, t - 1)
                    eng.dma_start(xt_sb[:, t % 2, :, :],
                                  xT_v[:, :, t * 128:(t + 1) * 128]).then_inc(sW, 16)

            @blk.tensor
            def _(eng):
                for t in range(NT):
                    eng.wait_ge(sW, 96 + 16 * (t + 1))
                    if t >= 1:
                        eng.wait_ge(sVEA, t)
                    mm = None
                    for dc in range(8):
                        lhs = xt_sb[:, t % 2, dc, :]
                        for jj in range(2):
                            eng.matmul(ps_qk[:, jj * 512:(jj + 1) * 512], lhs,
                                       wq_sb[:, dc, jj * 512:(jj + 1) * 512],
                                       start=(dc == 0), stop=(dc == 7))
                            eng.matmul(ps_qk[:, 1024 + jj * 512:1024 + (jj + 1) * 512], lhs,
                                       wk_sb[:, dc, jj * 512:(jj + 1) * 512],
                                       start=(dc == 0), stop=(dc == 7))
                            mm = eng.matmul(ps_v[:, jj * 512:(jj + 1) * 512], lhs,
                                            wv_sb[:, dc, jj * 512:(jj + 1) * 512],
                                            start=(dc == 0), stop=(dc == 7))
                    mm.then_inc(sMMA, 1)

            @blk.scalar
            def _(eng):
                for t in range(NT):
                    eng.wait_ge(sMMA, t + 1)
                    a = None
                    for h in range(H):
                        eng.activation(sq_scr[:, :], ps_qk[:, h * DH:(h + 1) * DH],
                                       AF.Square, accum_out=ssq[:, h:h + 1])
                    for h in range(H):
                        a = eng.activation(sq_scr[:, :],
                                           ps_qk[:, 1024 + h * DH:1024 + (h + 1) * DH],
                                           AF.Square, accum_out=ssq[:, H + h:H + h + 1])
                    a.then_inc(sSQ2, 1)
                    eng.wait_ge(sSQ2, t + 1)
                    ach_n[0] += 1
                    eng.activation(rn_sb[:, :], ssq[:, :], AF.Ln,
                                   bias=eps_sb[:, 1:2]).then_inc(sACH, 1)
                    eng.wait_ge(sACH, ach_n[0])
                    eng.activation(rn_sb[:, :], rn_sb[:, :], AF.Exp,
                                   scale=-0.5).then_inc(sRN, 1)

            @blk.vector
            def _(eng):
                eng.memset(vaug[:, :, :, :], 1.0)
                for t in range(NT):
                    eng.wait_ge(sRN, t + 1)
                    if t >= 2:
                        eng.wait_ge(sSCR, 32 * (t - 1))
                    for h in range(H):
                        eng.tensor_scalar(qn_t[:, t % 2, h * DH:(h + 1) * DH],
                                          ps_qk[:, h * DH:(h + 1) * DH],
                                          rn_sb[:, h:h + 1], None, op0=ALU.mult)
                    for h in range(H):
                        eng.tensor_scalar(kn_t[:, t % 2, h * DH:(h + 1) * DH],
                                          ps_qk[:, 1024 + h * DH:1024 + (h + 1) * DH],
                                          rn_sb[:, H + h:H + h + 1], None, op0=ALU.mult)
                    eng.tensor_copy(vaug[:, t, :, 0:DH],
                                    ps_v.rearrange("p (h d) -> p h d", h=H)[:, :, :]
                                    ).then_inc(sVEA, 1)

            @blk.sync
            def _(eng):
                for t in range(NT):
                    eng.wait_ge(sVEA, t + 1)
                    eng.dma_start(qn_v[:, t, :], qn_t[:, t % 2, :]).then_inc(sSCR, 16)
                    eng.dma_start(kn_v[:, t, :], kn_t[:, t % 2, :]).then_inc(sSCR, 16)

        pa.close()

        pb = contextlib.ExitStack()
        knT = pb.enter_context(nc.sbuf_tensor("knT", [128, 8, S], BF16))
        qnT = pb.enter_context(nc.sbuf_tensor("qnT", [128, 2, 8, 256], BF16))
        e_sb = pb.enter_context(nc.sbuf_tensor("e_sb", [128, 2, 1024], BF16))
        attn = pb.enter_context(nc.sbuf_tensor("attn", [128, 2, E], F32))
        xres = pb.enter_context(nc.sbuf_tensor("xres", [128, 2, E], F32))
        x1bt = pb.enter_context(nc.sbuf_tensor("x1bt", [128, 2, E], BF16))
        rec = pb.enter_context(nc.sbuf_tensor("rec", [128, 4], F32))
        lnst = pb.enter_context(nc.sbuf_tensor("lnst", [128, 16], F32))
        sq_b = pb.enter_context(nc.sbuf_tensor("sq_b", [128, E], F32))
        ps_sp = pb.enter_context(nc.psum_tensor("ps_sp", [128, 2048], F32))
        ps_o = [pb.enter_context(nc.psum_tensor(f"ps_o{i}", [128, 512], F32)) for i in range(4)]

        NQ2 = 16

        with nc.Block() as blk:
            @blk.sync
            def _(eng):
                eng.wait_ge(sSCR, 32 * NT)
                eng.wait_ge(sMMA, NT)
                for c in range(8):
                    eng.dma_start(knT[:, c, :], scr_kn[:, c * 128:(c + 1) * 128],
                                  transpose=True).then_inc(sTRK, 16)
                for qt in range(NQ2):
                    if qt >= 2:
                        eng.wait_ge(sQD, qt - 1)
                    for c in range(8):
                        eng.dma_start(qnT[:, qt % 2, c, :],
                                      scr_qn[qt * 256:(qt + 1) * 256, c * 128:(c + 1) * 128],
                                      transpose=True).then_inc(sTRQ, 16)
                    if qt >= 1:
                        eng.wait_ge(sLNV, qt)
                    eng.dma_start(xres[:, :, :], x_v[:, qt * 2:(qt + 1) * 2, :]
                                  ).then_inc(sXRE, 16)
                    if qt >= 1:
                        eng.dma_start(x1f_v[:, (qt - 1) * 2:qt * 2, :], attn[:, :, :]
                                      ).then_inc(sX1W, 16)
                        eng.dma_start(x1b_v[:, (qt - 1) * 2:qt * 2, :], x1bt[:, :, :]
                                      ).then_inc(sX1W, 16)
                    if qt == 8:
                        eng.wait_ge(sX1W, 256)
                        for c in range(8):
                            eng.dma_start(x1T0[:, c, :],
                                          scr_x1b[0:2048, c * 128:(c + 1) * 128],
                                          transpose=True).then_inc(sXT1, 16)
                eng.wait_ge(sLNV, NQ2)
                eng.dma_start(x1f_v[:, 30:32, :], attn[:, :, :]).then_inc(sX1W, 16)
                eng.dma_start(x1b_v[:, 30:32, :], x1bt[:, :, :]).then_inc(sX1W, 16)

            @blk.tensor
            def _(eng):
                eng.wait_ge(sTRK, 16 * 8)
                for qt in range(NQ2):
                    eng.wait_ge(sTRQ, 128 * (qt + 1))
                    for hc in range(8):
                        ph = qt * 8 + hc

                        def av_unit(v, last):
                            vn = ph * 16 + v
                            eng.wait_ge(sEX, vn + 1)
                            if v == 0 and ph >= 1:
                                eng.wait_ge(sEP, ph)
                            mm = None
                            for j in range(2):
                                kc = 2 * v + j
                                for hip in range(2):
                                    for sub in range(2):
                                        col = hip * 512 + j * 256 + sub * 128
                                        mm = eng.matmul(
                                            ps_o[hip * 2 + sub][:, 0:DH + 1],
                                            e_sb[:, vn % 2, col:col + 128],
                                            vaug[:, kc, 2 * hc + hip, :],
                                            start=(kc == 0), stop=(last and kc == KC - 1))
                            mm.then_inc(sAV, 1)

                        for u in range(16):
                            un = ph * 16 + u
                            if un >= 2:
                                eng.wait_ge(sEX, un - 1)
                            base = (u % 2) * 1024
                            for j in range(2):
                                kc = 2 * u + j
                                for hip in range(2):
                                    col = base + hip * 512 + j * 256
                                    mm = eng.matmul(
                                        ps_sp[:, col:col + 256],
                                        knT[hip * DH:(hip + 1) * DH, hc, kc * 128:(kc + 1) * 128],
                                        qnT[hip * DH:(hip + 1) * DH, qt % 2, hc, :],
                                        start=True, stop=True)
                            mm.then_inc(sSC, 1)
                            if u >= 1:
                                av_unit(u - 1, last=False)
                        av_unit(15, last=True)
                    eng.nop().then_inc(sQD, 1)

            @blk.scalar
            def _(eng):
                eng.wait_ge(sW, 16)
                for qt in range(NQ2):
                    for hc in range(8):
                        ph = qt * 8 + hc
                        for u in range(16):
                            un = ph * 16 + u
                            eng.wait_ge(sSC, un + 1)
                            if un >= 2:
                                eng.wait_ge(sAV, un - 1)
                            eng.activation(e_sb[:, un % 2, :],
                                           ps_sp[:, (u % 2) * 1024:(u % 2 + 1) * 1024],
                                           AF.Exp, scale=g_sb[:, 0:1]).then_inc(sEX, 1)
                    eng.wait_ge(sLN2, qt + 1)
                    a = None
                    for sub in range(2):
                        a = eng.activation(sq_b[:, :], xres[:, sub, :], AF.Square,
                                           accum_out=lnst[:, 8 + sub:9 + sub])
                    a.then_inc(sLN3, 1)
                    eng.wait_ge(sLN3, qt + 1)
                    ach_n[0] += 1
                    eng.activation(lnst[:, 12:14], lnst[:, 8:10], AF.Ln,
                                   scale=1.0 / E, bias=eps_sb[:, 0:1]).then_inc(sACH, 1)
                    eng.wait_ge(sACH, ach_n[0])
                    eng.activation(lnst[:, 12:14], lnst[:, 12:14], AF.Exp,
                                   scale=-0.5).then_inc(sLN4, 1)

            @blk.vector
            def _(eng):
                for qt in range(NQ2):
                    for hc in range(8):
                        ph = qt * 8 + hc
                        eng.wait_ge(sAV, 16 * (ph + 1))
                        r = None
                        for i in range(4):
                            r = eng.reciprocal(rec[:, i:i + 1], ps_o[i][:, DH:DH + 1])
                        r.then_inc(sREC, 1)
                        eng.wait_ge(sREC, ph + 1)
                        if hc == 0 and qt >= 1:
                            eng.wait_ge(sX1W, 32 * qt)
                        t_ = None
                        for hip in range(2):
                            for sub in range(2):
                                i = hip * 2 + sub
                                t_ = eng.tensor_scalar(
                                    attn[:, sub, (2 * hc + hip) * DH:(2 * hc + hip + 1) * DH],
                                    ps_o[i][:, 0:DH], rec[:, i:i + 1], None, op0=ALU.mult)
                        t_.then_inc(sEP, 1)
                    eng.wait_ge(sXRE, 16 * (qt + 1))
                    if qt >= 1:
                        eng.wait_ge(sX1W, 32 * qt)
                    vch_n[0] += 1
                    eng.tensor_tensor(attn[:, :, :], attn[:, :, :], xres[:, :, :],
                                      op=ALU.add).then_inc(sVCH, 1)
                    eng.wait_ge(sVCH, vch_n[0])
                    eng.reduce_sum(lnst[:, 0:2].rearrange("p (c n) -> p c n", c=2)[:, :, :],
                                   attn[:, :, :], axis=AX.X).then_inc(sLN1, 1)
                    eng.wait_ge(sLN1, qt + 1)
                    eng.tensor_scalar(lnst[:, 4:6], lnst[:, 0:2], 1.0 / E, None,
                                      op0=ALU.mult).then_inc(sLN1B, 1)
                    eng.wait_ge(sLN1B, qt + 1)
                    t_ = None
                    for sub in range(2):
                        t_ = eng.tensor_scalar(xres[:, sub, :], attn[:, sub, :],
                                               lnst[:, 4 + sub:5 + sub], None,
                                               op0=ALU.subtract)
                    t_.then_inc(sLN2, 1)
                    eng.wait_ge(sLN4, qt + 1)
                    t_ = None
                    for sub in range(2):
                        t_ = eng.tensor_scalar(attn[:, sub, :], xres[:, sub, :],
                                               lnst[:, 12 + sub:13 + sub], None, op0=ALU.mult)
                    vch_n[0] += 1
                    t_.then_inc(sVCH, 1)
                    eng.wait_ge(sVCH, vch_n[0])
                    eng.tensor_copy(x1bt[:, :, :], attn[:, :, :]).then_inc(sLNV, 1)

        pb.close()
        ab.close()


        pc = contextlib.ExitStack()
        w1s = [pc.enter_context(nc.sbuf_tensor(f"w1s{k}", [128, 8, 1024], BF16)) for k in range(2)]
        w2s = [pc.enter_context(nc.sbuf_tensor(f"w2s{k}", [128, 8, 1024], BF16)) for k in range(2)]
        x1T1 = pc.enter_context(nc.sbuf_tensor("x1T1", [128, 8, 2048], BF16))
        acc = pc.enter_context(nc.sbuf_tensor("acc", [128, 16, E], BF16))
        hT = pc.enter_context(nc.sbuf_tensor("hT", [128, 8, 512], BF16))
        gate = pc.enter_context(nc.sbuf_tensor("gate", [128, 2, 16, NE], F32))
        gsc = pc.enter_context(nc.sbuf_tensor("gsc", [128, 64], F32))
        xf = pc.enter_context(nc.sbuf_tensor("xf", [128, E], F32))
        sfin = pc.enter_context(nc.sbuf_tensor("sfin", [128, E], F32))
        cent = pc.enter_context(nc.sbuf_tensor("cent", [128, E], F32))
        outb = pc.enter_context(nc.sbuf_tensor("outb", [128, E], F32))
        sq_c = pc.enter_context(nc.sbuf_tensor("sq_c", [128, E], F32))
        fst = pc.enter_context(nc.sbuf_tensor("fst", [128, 8], F32))
        ps_h = [pc.enter_context(nc.psum_tensor(f"ps_h{i}", [128, 512], F32)) for i in range(2)]
        ps_y = [pc.enter_context(nc.psum_tensor(f"ps_y{i}", [128, 512], F32)) for i in range(4)]

        def combo(i):
            return i // 32, (i % 32) // 4, (i % 32) % 4

        with nc.Block() as blk:
            @blk.gpsimd
            def _(eng):
                for i in range(NCOMB):
                    half, e, dq = combo(i)
                    if i == 0:
                        eng.wait_ge(sX1W, 512)
                    if i >= 2:
                        eng.wait_ge(sWDN, i - 1)
                    k = i % 2
                    eng.dma_start(w1s[k][:, :, :],
                                  w1T_v[e, :, :, dq * 1024:(dq + 1) * 1024]
                                  ).then_inc(sWFF, 16)
                    eng.dma_start(w2s[k][:, :, :],
                                  w2T_v[e, :, dq * 8:(dq + 1) * 8, :]
                                  ).then_inc(sWFF, 16)

            @blk.sync
            def _(eng):
                eng.wait_ge(sX1W, 512)
                for c in range(8):
                    eng.dma_start(x1T1[:, c, :],
                                  scr_x1b[2048:4096, c * 128:(c + 1) * 128],
                                  transpose=True).then_inc(sXT1, 16)
                for half in range(2):
                    for tt in range(16):
                        g = half * 16 + tt
                        eng.wait_ge(sACC, 1024 * (half + 1))
                        if g >= 1:
                            eng.wait_ge(sFC, g)
                        eng.dma_start(xf[:, :], x1f_v[:, g, :]).then_inc(sXF, 16)
                        if tt >= 1:
                            eng.wait_ge(sFV, g)
                            eng.dma_start(out_v[:, g - 1, :], outb[:, :]
                                          ).then_inc(sOUT, 16)
                    eng.wait_ge(sFV, (half + 1) * 16)
                    eng.dma_start(out_v[:, half * 16 + 15, :], outb[:, :]
                                  ).then_inc(sOUT, 16)
                eng.wait_ge(sOUT, 16 * 32)

            @blk.tensor
            def _(eng):
                for half in range(2):
                    eng.wait_ge(sXT1, 128 * (half + 1))
                    eng.wait_ge(sW, 96)
                    if half == 1:
                        eng.wait_ge(sSILU, 1024)
                    for tt in range(16):
                        if tt >= 2:
                            eng.wait_ge(sGATE, 16 * half + tt - 1)
                        mm = None
                        for dc in range(8):
                            mm = eng.matmul(ps_h[tt % 2][:, 0:NE],
                                            (x1T0 if half == 0 else x1T1)[:, dc, tt * 128:(tt + 1) * 128],
                                            wr_sb[:, dc, :],
                                            start=(dc == 0), stop=(dc == 7))
                        mm.then_inc(sRT, 1)
                    for i in range(half * 32, (half + 1) * 32):
                        _, e, dq = combo(i)
                        k = i % 2
                        eng.wait_ge(sWFF, 32 * (i + 1))
                        if i % 32 == 0:
                            eng.wait_ge(sGATE, 16 * (half + 1))
                        for tg in range(4):
                            gidx = i * 4 + tg
                            for c in range(8):
                                gs = gidx * 8 + c
                                if gs >= 2:
                                    eng.wait_ge(sSILU, gs - 1)
                                mm = None
                                for dc in range(8):
                                    mm = eng.matmul(ps_h[c % 2][:, :],
                                                    w1s[k][:, dc, c * 128:(c + 1) * 128],
                                                    (x1T0 if half == 0 else x1T1)[:, dc, tg * 512:(tg + 1) * 512],
                                                    start=(dc == 0), stop=(dc == 7))
                                mm.then_inc(sHMM, 1)
                            for dh2 in range(2):
                                ga = gidx * 2 + dh2
                                mm = None
                                for sub in range(4):
                                    if ga >= 1:
                                        eng.wait_ge(sACC, 4 * (ga - 1) + sub + 1)
                                    for c in range(8):
                                        if dh2 == 0 and sub == 0:
                                            eng.wait_ge(sSILU, 8 * gidx + c + 1)
                                        mm = eng.matmul(ps_y[sub][:, :],
                                                        hT[:, c, sub * 128:(sub + 1) * 128],
                                                        w2s[k][:, c, dh2 * 512:(dh2 + 1) * 512],
                                                        start=(c == 0), stop=(c == 7))
                                mm.then_inc(sYMM, 1)
                        eng.nop().then_inc(sWDN, 1)

            @blk.scalar
            def _(eng):
                def gexps(half):
                    for tt in range(16):
                        g = 16 * half + tt
                        eng.wait_ge(sG2, g + 1)
                        eng.activation(gsc[:, 25:26], gsc[:, 24:25], AF.Exp
                                       ).then_inc(sG3, 1)
                        if debug:
                            eng.wait_ge(sGATE, g + 1)
                            eng.dma_start(dbg_gsc[g * 128:(g + 1) * 128, :],
                                          gsc[:, :]).then_inc(sDBG2, 16)

                def silus(half):
                    if debug and half == 1:
                        eng.wait_ge(sACC, 1024)
                        eng.dma_start(dbg_acc[0:128, :],
                                      acc.rearrange("p a b -> p (a b)")[:, :]
                                      ).then_inc(sDBG, 16)
                    for i in range(half * 32, (half + 1) * 32):
                        for tg in range(4):
                            gidx = i * 4 + tg
                            for c in range(8):
                                gs = gidx * 8 + c
                                eng.wait_ge(sHMM, gs + 1)
                                if c == 0 and gidx >= 1:
                                    eng.wait_ge(sYMM, 2 * gidx)
                                eng.activation(hT[:, c, :], ps_h[c % 2][:, :],
                                               AF.Silu).then_inc(sSILU, 1)

                def lntail_act(half):
                    if debug:
                        eng.wait_ge(sGATE, 16 * (half + 1))
                        eng.dma_start(dbg_gate[half * 128:(half + 1) * 128, :],
                                      gate.rearrange("p h a b -> p h (a b)")[:, half, :]
                                      ).then_inc(sDBG, 16)
                        if half == 1:
                            eng.wait_ge(sACC, 2048)
                            eng.dma_start(dbg_acc[128:256, :],
                                          acc.rearrange("p a b -> p (a b)")[:, :]
                                          ).then_inc(sDBG, 16)
                    for tt in range(16):
                        g = half * 16 + tt
                        eng.wait_ge(sF2, g + 1)
                        eng.activation(sq_c[:, :], cent[:, :], AF.Square,
                                       accum_out=fst[:, 0:1]).then_inc(sF3, 1)
                        eng.wait_ge(sF3, g + 1)
                        ach_n[0] += 1
                        eng.activation(fst[:, 1:2], fst[:, 0:1], AF.Ln,
                                       scale=1.0 / E, bias=eps_sb[:, 0:1]).then_inc(sACH, 1)
                        eng.wait_ge(sACH, ach_n[0])
                        eng.activation(fst[:, 2:3], fst[:, 1:2], AF.Exp,
                                       scale=-0.5).then_inc(sF4, 1)

                gexps(0)
                silus(0)
                gexps(1)
                lntail_act(0)
                silus(1)
                lntail_act(1)

            @blk.vector
            def _(eng):
                def gates(half):
                    for tt in range(16):
                        g = 16 * half + tt
                        eng.wait_ge(sRT, g + 1)
                        if g >= 1:
                            eng.wait_ge(sG3, g)
                        if debug:
                            if g >= 1:
                                eng.wait_ge(sDBG2, 16 * g)
                            eng.tensor_copy(gsc[:, 56:64], ps_h[tt % 2][:, 0:NE])
                        eng.reduce_max(gsc[:, 16:17], ps_h[tt % 2][:, 0:NE],
                                       axis=AX.X).then_inc(sG1, 1)
                        eng.wait_ge(sG1, g + 1)
                        def VC(instr):
                            vch_n[0] += 1
                            instr.then_inc(sVCH, 1)
                            eng.wait_ge(sVCH, vch_n[0])
                        VC(eng.tensor_scalar(gsc[:, 0:NE], ps_h[tt % 2][:, 0:NE],
                                             gsc[:, 16:17], None, op0=ALU.subtract))
                        VC(eng.tensor_scalar(gsc[:, 8:8 + NE], gsc[:, 0:NE], 0.0, None,
                                             op0=ALU.is_equal))
                        VC(eng.scalar_tensor_tensor(gsc[:, 32:32 + NE], gsc[:, 8:8 + NE],
                                                    -BIG, gsc[:, 0:NE],
                                                    op0=ALU.mult, op1=ALU.add))
                        eng.reduce_max(gsc[:, 24:25], gsc[:, 32:32 + NE],
                                       axis=AX.X).then_inc(sG2, 1)
                        eng.wait_ge(sG3, g + 1)
                        VC(eng.tensor_scalar(gsc[:, 26:27], gsc[:, 25:26], 1.0, None,
                                             op0=ALU.add))
                        eng.reciprocal(gsc[:, 27:28], gsc[:, 26:27]).then_inc(sG4, 1)
                        eng.wait_ge(sG4, g + 1)
                        VC(eng.tensor_tensor(gsc[:, 28:29], gsc[:, 25:26], gsc[:, 27:28],
                                             op=ALU.mult))
                        VC(eng.tensor_scalar(gsc[:, 40:40 + NE], gsc[:, 32:32 + NE],
                                             gsc[:, 24:25], None, op0=ALU.is_equal))
                        VC(eng.tensor_scalar(gsc[:, 48:48 + NE], gsc[:, 8:8 + NE],
                                             gsc[:, 27:28], None, op0=ALU.mult))
                        eng.scalar_tensor_tensor(gate[:, half, tt, :], gsc[:, 40:40 + NE],
                                                 gsc[:, 28:29], gsc[:, 48:48 + NE],
                                                 op0=ALU.mult, op1=ALU.add
                                                 ).then_inc(sGATE, 1)

                def stts(half):
                    if debug and half == 1:
                        eng.wait_ge(sDBG, 32)
                    eng.memset(acc[:, :, :], 0.0).then_inc(sMS, 1)
                    for i in range(half * 32, (half + 1) * 32):
                        _, e, dq = combo(i)
                        for tg in range(4):
                            gidx = i * 4 + tg
                            for dh2 in range(2):
                                ga = gidx * 2 + dh2
                                eng.wait_ge(sYMM, ga + 1)
                                if i % 32 == 0 and tg == 0 and dh2 == 0:
                                    eng.wait_ge(sMS, half + 1)
                                for sub in range(4):
                                    tt = tg * 4 + sub
                                    eng.scalar_tensor_tensor(
                                        acc[:, tt, dh2 * 512:(dh2 + 1) * 512],
                                        ps_y[sub][:, :],
                                        gate[:, half, tt, e:e + 1],
                                        acc[:, tt, dh2 * 512:(dh2 + 1) * 512],
                                        op0=ALU.mult, op1=ALU.add).then_inc(sACC, 1)

                def lntail(half):
                    for tt in range(16):
                        g = half * 16 + tt
                        eng.wait_ge(sXF, 16 * (g + 1))
                        vch_n[0] += 1
                        eng.tensor_tensor(sfin[:, :], xf[:, :], acc[:, tt, :],
                                          op=ALU.add).then_inc(sVCH, 1)
                        eng.wait_ge(sVCH, vch_n[0])
                        eng.nop().then_inc(sFC, 1)
                        eng.reduce_sum(fst[:, 4:5], sfin[:, :], axis=AX.X
                                       ).then_inc(sF1, 1)
                        eng.wait_ge(sF1, g + 1)
                        eng.tensor_scalar(fst[:, 5:6], fst[:, 4:5], 1.0 / E, None,
                                          op0=ALU.mult).then_inc(sF1B, 1)
                        eng.wait_ge(sF1B, g + 1)
                        eng.tensor_scalar(cent[:, :], sfin[:, :], fst[:, 5:6], None,
                                          op0=ALU.subtract).then_inc(sF2, 1)
                        eng.wait_ge(sF4, g + 1)
                        if g >= 1:
                            eng.wait_ge(sOUT, 16 * g)
                        eng.tensor_scalar(outb[:, :], cent[:, :], fst[:, 2:3], None,
                                          op0=ALU.mult).then_inc(sFV, 1)

                gates(0)
                stts(0)
                gates(1)
                lntail(0)
                stts(1)
                lntail(1)

        pc.close()
    return nc


def kernel(L0_attn_x, L0_attn_Wq, L0_attn_Wk, L0_attn_Wv, L0_attn_g_scale,
           L0_ffn_W1, L0_ffn_W2, L0_ffn_Wr):
    x = np.ascontiguousarray(np.asarray(L0_attn_x, dtype=np.float32))
    wq = np.asarray(L0_attn_Wq, dtype=np.float32)
    wk = np.asarray(L0_attn_Wk, dtype=np.float32)
    wv = np.asarray(L0_attn_Wv, dtype=np.float32)
    wr = np.asarray(L0_ffn_Wr, dtype=np.float32)
    w1 = np.asarray(L0_ffn_W1, dtype=np.float32)
    w2 = np.asarray(L0_ffn_W2, dtype=np.float32)
    g = np.asarray(L0_attn_g_scale, dtype=np.float32)

    wqT = np.ascontiguousarray(wq.T)
    wkT = np.ascontiguousarray(wk.T)
    wvT = np.ascontiguousarray(wv.T)
    wrT = np.ascontiguousarray(wr.T)
    w1T = np.ascontiguousarray(w1.transpose(0, 2, 1))
    w2T = np.ascontiguousarray(w2.transpose(0, 2, 1))
    g128 = np.ascontiguousarray(np.broadcast_to(g.reshape(1, 1), (128, 1)))
    eps128 = np.ascontiguousarray(np.broadcast_to(np.array([1e-5, 1e-24], dtype=np.float32), (128, 2)))

    nc = build_nc()
    in_maps = []
    for c in range(8):
        in_maps.append({
            "xin": np.ascontiguousarray(x[c]),
            "xtin": np.ascontiguousarray(x[c].T),
            "wqT": wqT, "wkT": wkT, "wvT": wvT, "wrT": wrT,
            "g128": g128, "eps128": eps128, "w1T": w1T, "w2T": w2T,
        })
    res = run_bass_kernel_spmd(nc, in_maps, core_ids=list(range(8)))
    return np.stack([r["out"] for r in res.results], axis=0)


# version 17
# speedup vs baseline: 1.1238x; 1.0796x over previous
import numpy as np
import contextlib
import concourse.bass as bass
import concourse.mybir as mybir
from concourse.bass_utils import run_bass_kernel_spmd

F32 = mybir.dt.float32
BF16 = mybir.dt.bfloat16
AF = mybir.ActivationFunctionType
ALU = mybir.AluOpType
AX = mybir.AxisListType

S = 4096
E = 1024
H = 16
DH = 64
NT = S // 128
NQT = 8
KC = 32
NE = 8
DFF = 4096
NCOMB = 64
EPS_LN = 1e-5
BIG = 1.0e30


def build_nc(debug=False):
    nc = bass.Bass("TRN2", target_bir_lowering=False)

    x_d = nc.dram_tensor("xin", [S, E], F32, kind="ExternalInput")
    xT_d = nc.dram_tensor("xtin", [E, S], F32, kind="ExternalInput")
    wqT_d = nc.dram_tensor("wqT", [E, E], F32, kind="ExternalInput")
    wkT_d = nc.dram_tensor("wkT", [E, E], F32, kind="ExternalInput")
    wvT_d = nc.dram_tensor("wvT", [E, E], F32, kind="ExternalInput")
    wrT_d = nc.dram_tensor("wrT", [E, NE], F32, kind="ExternalInput")
    g_d = nc.dram_tensor("g128", [128, 1], F32, kind="ExternalInput")
    id_d = nc.dram_tensor("ident", [128, 128], F32, kind="ExternalInput")
    eps_d = nc.dram_tensor("eps128", [128, 2], F32, kind="ExternalInput")
    w1T_d = nc.dram_tensor("w1T", [NE, E, DFF], F32, kind="ExternalInput")
    w2T_d = nc.dram_tensor("w2T", [NE, DFF, E], F32, kind="ExternalInput")
    out_d = nc.dram_tensor("out", [S, E], F32, kind="ExternalOutput")
    dbg = "ExternalOutput" if debug else "Internal"
    scr_qn = nc.dram_tensor("scr_qn", [S, E], BF16, kind=dbg)
    scr_kn = nc.dram_tensor("scr_kn", [S, E], BF16, kind=dbg)
    scr_x1f = nc.dram_tensor("scr_x1f", [S, E], F32, kind=dbg)
    scr_x1b = nc.dram_tensor("scr_x1b", [S, E], BF16, kind=dbg)
    if debug:
        dbg_gate = nc.dram_tensor("dbg_gate", [256, 128], F32, kind="ExternalOutput")
        dbg_gsc = nc.dram_tensor("dbg_gsc", [4096, 64], F32, kind="ExternalOutput")
        dbg_acc = nc.dram_tensor("dbg_acc", [256, 16 * E], BF16, kind="ExternalOutput")

    xT_v = xT_d.rearrange("(c p) n -> p c n", p=128)
    wqT_v = wqT_d.rearrange("(c p) n -> p c n", p=128)
    wkT_v = wkT_d.rearrange("(c p) n -> p c n", p=128)
    wvT_v = wvT_d.rearrange("(c p) n -> p c n", p=128)
    wrT_v = wrT_d.rearrange("(c p) n -> p c n", p=128)
    x_v = x_d.rearrange("(t p) n -> p t n", p=128)
    out_v = out_d.rearrange("(t p) n -> p t n", p=128)
    qn_v = scr_qn.rearrange("(t p) n -> p t n", p=128)
    kn_v = scr_kn.rearrange("(t p) n -> p t n", p=128)
    x1f_v = scr_x1f.rearrange("(t p) n -> p t n", p=128)
    x1b_v = scr_x1b.rearrange("(t p) n -> p t n", p=128)
    w1T_v = w1T_d.rearrange("e (c p) f -> e p c f", p=128)
    w2T_v = w2T_d.rearrange("e (c p) f -> e p c f", p=128)

    ctx = contextlib.ExitStack()
    with ctx:
        g_sb = ctx.enter_context(nc.sbuf_tensor("g_sb", [128, 1], F32))
        eps_sb = ctx.enter_context(nc.sbuf_tensor("eps_sb", [128, 2], F32))
        wr_sb = ctx.enter_context(nc.sbuf_tensor("wr_sb", [128, NE, NE], BF16))
        id_sb = ctx.enter_context(nc.sbuf_tensor("id_sb", [128, 128], BF16))

        def sem(name):
            return ctx.enter_context(nc.semaphore(name))
        sW = sem("sW"); sMMA = sem("sMMA"); sSQ2 = sem("sSQ2"); sCL = sem("sCL")
        sRN = sem("sRN"); sVEA = sem("sVEA"); sSCR = sem("sSCR")
        sTRK = sem("sTRK"); sTRQ = sem("sTRQ"); sXRE = sem("sXRE")
        sSC = sem("sSC"); sEX = sem("sEX"); sAV = sem("sAV"); sQD = sem("sQD")
        sREC = sem("sREC"); sEP = sem("sEP"); sLN1 = sem("sLN1"); sLN2 = sem("sLN2")
        sLN3 = sem("sLN3"); sLN4 = sem("sLN4"); sLNV = sem("sLNV"); sX1W = sem("sX1W")
        sWFF = sem("sWFF"); sWDN = sem("sWDN"); sXT1 = sem("sXT1"); sX1D = sem("sX1D")
        sRT = sem("sRT"); sG1 = sem("sG1"); sG2 = sem("sG2"); sG3 = sem("sG3")
        sG4 = sem("sG4"); sGATE = sem("sGATE"); sMS = sem("sMS")
        sHMM = sem("sHMM"); sSILU = sem("sSILU"); sYMM = sem("sYMM"); sACC = sem("sACC")
        sXF = sem("sXF"); sF1 = sem("sF1"); sF2 = sem("sF2"); sF3 = sem("sF3")
        sF4 = sem("sF4"); sFV = sem("sFV"); sFC = sem("sFC"); sOUT = sem("sOUT")
        sLN1B = sem("sLN1B"); sF1B = sem("sF1B"); sG5 = sem("sG5"); sDBG = sem("sDBG"); sDBG2 = sem("sDBG2")
        sVCH = sem("sVCH"); vch_n = [0]
        sOC = sem("sOC"); sTP = sem("sTP"); sTQ = sem("sTQ"); tq_n = [0]
        sACH = sem("sACH"); ach_n = [0]

        x1T0 = ctx.enter_context(nc.sbuf_tensor("x1T0", [128, 8, 2048], BF16))
        ab = contextlib.ExitStack()
        vaug = ab.enter_context(nc.sbuf_tensor("vaug", [128, KC, H, DH + 1], BF16))
        pa = contextlib.ExitStack()
        wq_sb = pa.enter_context(nc.sbuf_tensor("wq_sb", [128, 8, E], BF16))
        wk_sb = pa.enter_context(nc.sbuf_tensor("wk_sb", [128, 8, E], BF16))
        wv_sb = pa.enter_context(nc.sbuf_tensor("wv_sb", [128, 8, E], BF16))
        xt_sb = pa.enter_context(nc.sbuf_tensor("xt_sb", [128, 4, 8, 128], BF16))
        qn_t = pa.enter_context(nc.sbuf_tensor("qn_t", [128, 2, E], BF16))
        kn_t = pa.enter_context(nc.sbuf_tensor("kn_t", [128, 2, E], BF16))
        ssq = pa.enter_context(nc.sbuf_tensor("ssq", [128, H], F32))
        rn_sb = pa.enter_context(nc.sbuf_tensor("rn_sb", [128, 2 * H], F32))
        sq_scr = pa.enter_context(nc.sbuf_tensor("sq_scr", [128, DH], F32))
        ps_a = [pa.enter_context(nc.psum_tensor(f"ps_a{i}", [128, 2048], F32)) for i in range(2)]

        with nc.Block() as blk:
            @blk.gpsimd
            def _(eng):
                eng.dma_start(g_sb[:, :], g_d[:, :]).then_inc(sW, 16)
                eng.dma_start(eps_sb[:, :], eps_d[:, :]).then_inc(sW, 16)
                eng.dma_start(wq_sb[:, :, :], wqT_v[:, :, :]).then_inc(sW, 16)
                eng.dma_start(wk_sb[:, :, :], wkT_v[:, :, :]).then_inc(sW, 16)
                eng.dma_start(wv_sb[:, :, :], wvT_v[:, :, :]).then_inc(sW, 16)
                eng.dma_start(wr_sb[:, :, :], wrT_v[:, :, :]).then_inc(sW, 16)
                eng.dma_start(id_sb[:, :], id_d[:, :]).then_inc(sW, 16)
                for i in range(2 * NT):
                    t = i % NT
                    if i >= 4:
                        eng.wait_ge(sMMA, i - 3)
                    eng.dma_start(xt_sb[:, i % 4, :, :],
                                  xT_v[:, :, t * 128:(t + 1) * 128]).then_inc(sW, 16)

            @blk.tensor
            def _(eng):
                for i in range(2 * NT):
                    t = i % NT
                    eng.wait_ge(sW, 112 + 16 * (i + 1))
                    if i >= 2:
                        eng.wait_ge(sVEA, i - 1)
                    mm = None
                    kv = i < NT
                    for dc in range(8):
                        lhs = xt_sb[:, i % 4, dc, :]
                        for jj in range(2):
                            if kv:
                                eng.matmul(ps_a[i % 2][:, jj * 512:(jj + 1) * 512], lhs,
                                           wk_sb[:, dc, jj * 512:(jj + 1) * 512],
                                           start=(dc == 0), stop=(dc == 7))
                                mm = eng.matmul(ps_a[i % 2][:, 1024 + jj * 512:1024 + (jj + 1) * 512],
                                                lhs, wv_sb[:, dc, jj * 512:(jj + 1) * 512],
                                                start=(dc == 0), stop=(dc == 7))
                            else:
                                mm = eng.matmul(ps_a[i % 2][:, jj * 512:(jj + 1) * 512], lhs,
                                                wq_sb[:, dc, jj * 512:(jj + 1) * 512],
                                                start=(dc == 0), stop=(dc == 7))
                    mm.then_inc(sMMA, 1)

            @blk.scalar
            def _(eng):
                for i in range(2 * NT):
                    eng.wait_ge(sMMA, i + 1)
                    if i >= 2:
                        eng.wait_ge(sVEA, i - 1)
                    a = None
                    for h in range(H):
                        a = eng.activation(sq_scr[:, :], ps_a[i % 2][:, h * DH:(h + 1) * DH],
                                           AF.Square, accum_out=ssq[:, h:h + 1])
                    a.then_inc(sSQ2, 1)
                    eng.wait_ge(sSQ2, i + 1)
                    ach_n[0] += 1
                    eng.activation(rn_sb[:, (i % 2) * H:(i % 2 + 1) * H], ssq[:, 0:H], AF.Ln,
                                   bias=eps_sb[:, 1:2]).then_inc(sACH, 1)
                    eng.wait_ge(sACH, ach_n[0])
                    eng.activation(rn_sb[:, (i % 2) * H:(i % 2 + 1) * H],
                                   rn_sb[:, (i % 2) * H:(i % 2 + 1) * H], AF.Exp,
                                   scale=-0.5).then_inc(sRN, 1)

            @blk.vector
            def _(eng):
                eng.memset(vaug[:, :, :, :], 1.0)
                for i in range(2 * NT):
                    t = i % NT
                    eng.wait_ge(sRN, i + 1)
                    if i >= 2:
                        eng.wait_ge(sSCR, 16 * (i - 1))
                    kv = i < NT
                    dst = kn_t if kv else qn_t
                    t_ = None
                    for h in range(H):
                        t_ = eng.tensor_scalar(dst[:, i % 2, h * DH:(h + 1) * DH],
                                               ps_a[i % 2][:, h * DH:(h + 1) * DH],
                                               rn_sb[:, (i % 2) * H + h:(i % 2) * H + h + 1],
                                               None, op0=ALU.mult)
                    if kv:
                        eng.tensor_copy(vaug[:, t, :, 0:DH],
                                        ps_a[i % 2][:, 1024:2048].rearrange(
                                            "p (h d) -> p h d", h=H)[:, :, :]
                                        ).then_inc(sVEA, 1)
                    else:
                        t_.then_inc(sVEA, 1)

            @blk.sync
            def _(eng):
                for i in range(2 * NT):
                    t = i % NT
                    eng.wait_ge(sVEA, i + 1)
                    if i < NT:
                        eng.dma_start(kn_v[:, t, :], kn_t[:, i % 2, :]).then_inc(sSCR, 16)
                    else:
                        eng.dma_start(qn_v[:, t, :], qn_t[:, i % 2, :]).then_inc(sSCR, 16)

        pa.close()

        pb = contextlib.ExitStack()
        knT = pb.enter_context(nc.sbuf_tensor("knT", [128, 8, S], BF16))
        qnT = pb.enter_context(nc.sbuf_tensor("qnT", [128, 2, 8, 256], BF16))
        e_sb = pb.enter_context(nc.sbuf_tensor("e_sb", [128, 2, 1024], BF16))
        attn = pb.enter_context(nc.sbuf_tensor("attn", [128, 2, E], F32))
        xres = pb.enter_context(nc.sbuf_tensor("xres", [128, 2, E], F32))
        x1bt = pb.enter_context(nc.sbuf_tensor("x1bt", [128, 2, E], BF16))
        rec = pb.enter_context(nc.sbuf_tensor("rec", [128, 4], F32))
        lnst = pb.enter_context(nc.sbuf_tensor("lnst", [128, 16], F32))
        sq_b = pb.enter_context(nc.sbuf_tensor("sq_b", [128, E], F32))
        ps_sp = pb.enter_context(nc.psum_tensor("ps_sp", [128, 2048], F32))
        ps_oa = [pb.enter_context(nc.psum_tensor(f"ps_oa{i}", [128, 512], F32)) for i in range(2)]
        ps_ot = [pb.enter_context(nc.psum_tensor(f"ps_ot{i}", [128, 1024], BF16)) for i in range(2)]
        oa_sb = pb.enter_context(nc.sbuf_tensor("oa_sb", [128, 2, 256], BF16))

        NQ2 = 16

        with nc.Block() as blk:
            @blk.sync
            def _(eng):
                eng.wait_ge(sSCR, 16 * 2 * NT)
                eng.wait_ge(sMMA, 2 * NT)
                for c in range(8):
                    eng.dma_start(knT[:, c, :], scr_kn[:, c * 128:(c + 1) * 128],
                                  transpose=True).then_inc(sTRK, 16)
                for qt in range(NQ2):
                    if qt >= 2:
                        eng.wait_ge(sQD, qt - 1)
                    for c in range(8):
                        eng.dma_start(qnT[:, qt % 2, c, :],
                                      scr_qn[qt * 256:(qt + 1) * 256, c * 128:(c + 1) * 128],
                                      transpose=True).then_inc(sTRQ, 16)
                    if qt >= 1:
                        eng.wait_ge(sLNV, qt)
                    eng.dma_start(xres[:, :, :], x_v[:, qt * 2:(qt + 1) * 2, :]
                                  ).then_inc(sXRE, 16)
                    if qt >= 1:
                        eng.dma_start(x1f_v[:, (qt - 1) * 2:qt * 2, :], attn[:, :, :]
                                      ).then_inc(sX1W, 16)
                        eng.dma_start(x1b_v[:, (qt - 1) * 2:qt * 2, :], x1bt[:, :, :]
                                      ).then_inc(sX1W, 16)
                    if qt == 8:
                        eng.wait_ge(sX1W, 256)
                        for c in range(8):
                            eng.dma_start(x1T0[:, c, :],
                                          scr_x1b[0:2048, c * 128:(c + 1) * 128],
                                          transpose=True).then_inc(sXT1, 16)
                eng.wait_ge(sLNV, NQ2)
                eng.dma_start(x1f_v[:, 30:32, :], attn[:, :, :]).then_inc(sX1W, 16)
                eng.dma_start(x1b_v[:, 30:32, :], x1bt[:, :, :]).then_inc(sX1W, 16)

            @blk.tensor
            def _(eng):
                eng.wait_ge(sTRK, 16 * 8)
                for qt in range(NQ2):
                    eng.wait_ge(sTRQ, 128 * (qt + 1))
                    for hc in range(8):
                        ph = qt * 8 + hc

                        def av_unit(v, last):
                            vn = ph * 16 + v
                            eng.wait_ge(sEX, vn + 1)
                            if v == 0 and ph >= 1:
                                eng.wait_ge(sOC, ph)
                            mm = None
                            for j in range(2):
                                kc = 2 * v + j
                                for hip in range(2):
                                    mm = eng.matmul(
                                        ps_oa[hip][0:DH + 1, 0:256],
                                        vaug[:, kc, 2 * hc + hip, :],
                                        e_sb[:, vn % 2, hip * 512 + j * 256:hip * 512 + (j + 1) * 256],
                                        start=(kc == 0), stop=(last and kc == KC - 1))
                            mm.then_inc(sAV, 1)

                        def otransp(php, sub):
                            if sub == 0:
                                eng.wait_ge(sOC, php + 1)
                                if php >= 1:
                                    eng.wait_ge(sEP, php)
                            else:
                                eng.wait_ge(sTQ, tq_n[0])
                            mm = None
                            for hip in range(2):
                                mm = eng.matmul(
                                    ps_ot[hip][:, sub * 128:sub * 128 + 65],
                                    oa_sb[0:DH + 1, hip, sub * 128:(sub + 1) * 128],
                                    id_sb[0:DH + 1, 0:DH + 1],
                                    start=True, stop=True, is_transpose=True)
                            if sub == 0:
                                tq_n[0] += 1
                                mm.then_inc(sTQ, 1)
                            else:
                                mm.then_inc(sTP, 1)

                        for u in range(16):
                            un = ph * 16 + u
                            if un >= 2:
                                eng.wait_ge(sEX, un - 1)
                            base = (u % 2) * 1024
                            for j in range(2):
                                kc = 2 * u + j
                                for hip in range(2):
                                    col = base + hip * 512 + j * 256
                                    mm = eng.matmul(
                                        ps_sp[:, col:col + 256],
                                        knT[hip * DH:(hip + 1) * DH, hc, kc * 128:(kc + 1) * 128],
                                        qnT[hip * DH:(hip + 1) * DH, qt % 2, hc, :],
                                        start=True, stop=True)
                            mm.then_inc(sSC, 1)
                            if u == 1 and ph >= 1:
                                otransp(ph - 1, 0)
                                otransp(ph - 1, 1)
                            if u >= 1:
                                av_unit(u - 1, last=False)
                        av_unit(15, last=True)
                        if ph == NQ2 * 8 - 1:
                            otransp(ph, 0)
                            otransp(ph, 1)
                    eng.nop().then_inc(sQD, 1)

            @blk.scalar
            def _(eng):
                eng.wait_ge(sW, 16)
                for qt in range(NQ2):
                    for hc in range(8):
                        ph = qt * 8 + hc
                        for u in range(16):
                            un = ph * 16 + u
                            eng.wait_ge(sSC, un + 1)
                            if un >= 2:
                                eng.wait_ge(sAV, un - 1)
                            eng.activation(e_sb[:, un % 2, :],
                                           ps_sp[:, (u % 2) * 1024:(u % 2 + 1) * 1024],
                                           AF.Exp, scale=g_sb[:, 0:1]).then_inc(sEX, 1)
                    eng.wait_ge(sLN2, qt + 1)
                    a = None
                    for sub in range(2):
                        a = eng.activation(sq_b[:, :], xres[:, sub, :], AF.Square,
                                           accum_out=lnst[:, 8 + sub:9 + sub])
                    a.then_inc(sLN3, 1)
                    eng.wait_ge(sLN3, qt + 1)
                    ach_n[0] += 1
                    eng.activation(lnst[:, 12:14], lnst[:, 8:10], AF.Ln,
                                   scale=1.0 / E, bias=eps_sb[:, 0:1]).then_inc(sACH, 1)
                    eng.wait_ge(sACH, ach_n[0])
                    eng.activation(lnst[:, 12:14], lnst[:, 12:14], AF.Exp,
                                   scale=-0.5).then_inc(sLN4, 1)

            @blk.vector
            def _(eng):
                for qt in range(NQ2):
                    for hc in range(8):
                        ph = qt * 8 + hc
                        eng.wait_ge(sAV, 16 * (ph + 1))
                        if ph >= 1:
                            eng.wait_ge(sTP, ph)
                        c_ = None
                        for hip in range(2):
                            c_ = eng.tensor_copy(oa_sb[0:DH + 1, hip, :],
                                                 ps_oa[hip][0:DH + 1, 0:256])
                        c_.then_inc(sOC, 1)
                        eng.wait_ge(sTP, ph + 1)
                        r = None
                        for hip in range(2):
                            for sub in range(2):
                                i = hip * 2 + sub
                                r = eng.reciprocal(rec[:, i:i + 1],
                                                   ps_ot[hip][:, sub * 128 + DH:sub * 128 + DH + 1])
                        r.then_inc(sREC, 1)
                        eng.wait_ge(sREC, ph + 1)
                        if hc == 0 and qt >= 1:
                            eng.wait_ge(sX1W, 32 * qt)
                        t_ = None
                        for hip in range(2):
                            for sub in range(2):
                                i = hip * 2 + sub
                                t_ = eng.tensor_scalar(
                                    attn[:, sub, (2 * hc + hip) * DH:(2 * hc + hip + 1) * DH],
                                    ps_ot[hip][:, sub * 128:sub * 128 + DH], rec[:, i:i + 1],
                                    None, op0=ALU.mult)
                        t_.then_inc(sEP, 1)
                    eng.wait_ge(sXRE, 16 * (qt + 1))
                    if qt >= 1:
                        eng.wait_ge(sX1W, 32 * qt)
                    vch_n[0] += 1
                    eng.tensor_tensor(attn[:, :, :], attn[:, :, :], xres[:, :, :],
                                      op=ALU.add).then_inc(sVCH, 1)
                    eng.wait_ge(sVCH, vch_n[0])
                    eng.reduce_sum(lnst[:, 0:2].rearrange("p (c n) -> p c n", c=2)[:, :, :],
                                   attn[:, :, :], axis=AX.X).then_inc(sLN1, 1)
                    eng.wait_ge(sLN1, qt + 1)
                    eng.tensor_scalar(lnst[:, 4:6], lnst[:, 0:2], 1.0 / E, None,
                                      op0=ALU.mult).then_inc(sLN1B, 1)
                    eng.wait_ge(sLN1B, qt + 1)
                    t_ = None
                    for sub in range(2):
                        t_ = eng.tensor_scalar(xres[:, sub, :], attn[:, sub, :],
                                               lnst[:, 4 + sub:5 + sub], None,
                                               op0=ALU.subtract)
                    t_.then_inc(sLN2, 1)
                    eng.wait_ge(sLN4, qt + 1)
                    t_ = None
                    for sub in range(2):
                        t_ = eng.tensor_scalar(attn[:, sub, :], xres[:, sub, :],
                                               lnst[:, 12 + sub:13 + sub], None, op0=ALU.mult)
                    vch_n[0] += 1
                    t_.then_inc(sVCH, 1)
                    eng.wait_ge(sVCH, vch_n[0])
                    eng.tensor_copy(x1bt[:, :, :], attn[:, :, :]).then_inc(sLNV, 1)

        pb.close()
        ab.close()


        pc = contextlib.ExitStack()
        w1s = [pc.enter_context(nc.sbuf_tensor(f"w1s{k}", [128, 8, 1024], BF16)) for k in range(2)]
        w2s = [pc.enter_context(nc.sbuf_tensor(f"w2s{k}", [128, 8, 1024], BF16)) for k in range(2)]
        x1T1 = pc.enter_context(nc.sbuf_tensor("x1T1", [128, 8, 2048], BF16))
        acc = pc.enter_context(nc.sbuf_tensor("acc", [128, 16, E], BF16))
        hT = pc.enter_context(nc.sbuf_tensor("hT", [128, 8, 512], BF16))
        gate = pc.enter_context(nc.sbuf_tensor("gate", [128, 2, 16, NE], F32))
        gsc = pc.enter_context(nc.sbuf_tensor("gsc", [128, 64], F32))
        xf = pc.enter_context(nc.sbuf_tensor("xf", [128, E], F32))
        sfin = pc.enter_context(nc.sbuf_tensor("sfin", [128, E], F32))
        cent = pc.enter_context(nc.sbuf_tensor("cent", [128, E], F32))
        outb = pc.enter_context(nc.sbuf_tensor("outb", [128, E], F32))
        sq_c = pc.enter_context(nc.sbuf_tensor("sq_c", [128, E], F32))
        fst = pc.enter_context(nc.sbuf_tensor("fst", [128, 8], F32))
        ps_h = [pc.enter_context(nc.psum_tensor(f"ps_h{i}", [128, 512], F32)) for i in range(2)]
        ps_y = [pc.enter_context(nc.psum_tensor(f"ps_y{i}", [128, 512], F32)) for i in range(4)]
        ps_r = [pc.enter_context(nc.psum_tensor(f"ps_r{i}", [128, 512], F32)) for i in range(2)]

        def combo(i):
            return i // 32, (i % 32) // 4, (i % 32) % 4

        with nc.Block() as blk:
            @blk.gpsimd
            def _(eng):
                for i in range(NCOMB):
                    half, e, dq = combo(i)
                    if i == 0:
                        eng.wait_ge(sX1W, 512)
                    if i >= 2:
                        eng.wait_ge(sWDN, i - 1)
                    k = i % 2
                    eng.dma_start(w1s[k][:, :, :],
                                  w1T_v[e, :, :, dq * 1024:(dq + 1) * 1024]
                                  ).then_inc(sWFF, 16)
                    eng.dma_start(w2s[k][:, :, :],
                                  w2T_v[e, :, dq * 8:(dq + 1) * 8, :]
                                  ).then_inc(sWFF, 16)

            @blk.sync
            def _(eng):
                eng.wait_ge(sX1W, 512)
                for c in range(8):
                    eng.dma_start(x1T1[:, c, :],
                                  scr_x1b[2048:4096, c * 128:(c + 1) * 128],
                                  transpose=True).then_inc(sXT1, 16)
                for half in range(2):
                    for tt in range(16):
                        g = half * 16 + tt
                        eng.wait_ge(sACC, 1024 * (half + 1))
                        if g >= 1:
                            eng.wait_ge(sFC, g)
                        eng.dma_start(xf[:, :], x1f_v[:, g, :]).then_inc(sXF, 16)
                        if tt >= 1:
                            eng.wait_ge(sFV, g)
                            eng.dma_start(out_v[:, g - 1, :], outb[:, :]
                                          ).then_inc(sOUT, 16)
                    eng.wait_ge(sFV, (half + 1) * 16)
                    eng.dma_start(out_v[:, half * 16 + 15, :], outb[:, :]
                                  ).then_inc(sOUT, 16)
                eng.wait_ge(sOUT, 16 * 32)

            @blk.tensor
            def _(eng):
                def router(half, only_tt=None):
                    for tt in ([only_tt] if only_tt is not None else range(16)):
                        if tt == 0:
                            eng.wait_ge(sXT1, 128 * (half + 1))
                            eng.wait_ge(sW, 112)
                        if tt >= 2:
                            eng.wait_ge(sGATE, 16 * half + tt - 1)
                        mm = None
                        for dc in range(8):
                            mm = eng.matmul(ps_r[tt % 2][:, 0:NE],
                                            (x1T0 if half == 0 else x1T1)[:, dc, tt * 128:(tt + 1) * 128],
                                            wr_sb[:, dc, :],
                                            start=(dc == 0), stop=(dc == 7))
                        mm.then_inc(sRT, 1)
                for half in range(2):
                    if half == 0:
                        router(0)
                    for i in range(half * 32, (half + 1) * 32):
                        if half == 0 and i == 17:
                            router(1)
                        _, e, dq = combo(i)
                        k = i % 2
                        eng.wait_ge(sWFF, 32 * (i + 1))
                        for tg in range(4):
                            gidx = i * 4 + tg
                            for c in range(8):
                                gs = gidx * 8 + c
                                if gs >= 2:
                                    eng.wait_ge(sSILU, gs - 1)
                                mm = None
                                for dc in range(8):
                                    mm = eng.matmul(ps_h[c % 2][:, :],
                                                    w1s[k][:, dc, c * 128:(c + 1) * 128],
                                                    (x1T0 if half == 0 else x1T1)[:, dc, tg * 512:(tg + 1) * 512],
                                                    start=(dc == 0), stop=(dc == 7))
                                mm.then_inc(sHMM, 1)
                            for dh2 in range(2):
                                ga = gidx * 2 + dh2
                                mm = None
                                for sub in range(4):
                                    if ga >= 1:
                                        eng.wait_ge(sACC, 4 * (ga - 1) + sub + 1)
                                    for c in range(8):
                                        if dh2 == 0 and sub == 0:
                                            eng.wait_ge(sSILU, 8 * gidx + c + 1)
                                        mm = eng.matmul(ps_y[sub][:, :],
                                                        hT[:, c, sub * 128:(sub + 1) * 128],
                                                        w2s[k][:, c, dh2 * 512:(dh2 + 1) * 512],
                                                        start=(c == 0), stop=(c == 7))
                                mm.then_inc(sYMM, 1)
                        eng.nop().then_inc(sWDN, 1)

            @blk.scalar
            def _(eng):
                def gexps(half, only_tt=None):
                    for tt in ([only_tt] if only_tt is not None else range(16)):
                        g = 16 * half + tt
                        eng.wait_ge(sG2, g + 1)
                        eng.activation(gsc[:, 25:26], gsc[:, 24:25], AF.Exp
                                       ).then_inc(sG3, 1)
                        if debug:
                            eng.wait_ge(sGATE, g + 1)
                            eng.dma_start(dbg_gsc[g * 128:(g + 1) * 128, :],
                                          gsc[:, :]).then_inc(sDBG2, 16)

                def silus(half):
                    if debug and half == 1:
                        eng.wait_ge(sACC, 1024)
                        eng.dma_start(dbg_acc[0:128, :],
                                      acc.rearrange("p a b -> p (a b)")[:, :]
                                      ).then_inc(sDBG, 16)
                    for i in range(half * 32, (half + 1) * 32):
                        if half == 0 and i == 17:
                            gexps(1)
                        if half == 1 and i == 32:
                            for _tt in range(16):
                                lntail_act_tile(0, _tt)
                        for tg in range(4):
                            gidx = i * 4 + tg
                            for c in range(8):
                                gs = gidx * 8 + c
                                eng.wait_ge(sHMM, gs + 1)
                                if c == 0 and gidx >= 1:
                                    eng.wait_ge(sYMM, 2 * gidx)
                                eng.activation(hT[:, c, :], ps_h[c % 2][:, :],
                                               AF.Silu).then_inc(sSILU, 1)

                def lntail_act_tile(half, tt):
                    g = half * 16 + tt
                    eng.wait_ge(sF2, g + 1)
                    eng.activation(sq_c[:, :], cent[:, :], AF.Square,
                                   accum_out=fst[:, 0:1]).then_inc(sF3, 1)
                    eng.wait_ge(sF3, g + 1)
                    ach_n[0] += 1
                    eng.activation(fst[:, 1:2], fst[:, 0:1], AF.Ln,
                                   scale=1.0 / E, bias=eps_sb[:, 0:1]).then_inc(sACH, 1)
                    eng.wait_ge(sACH, ach_n[0])
                    eng.activation(fst[:, 2:3], fst[:, 1:2], AF.Exp,
                                   scale=-0.5).then_inc(sF4, 1)

                def lntail_act(half):
                    if debug:
                        eng.wait_ge(sGATE, 16 * (half + 1))
                        eng.dma_start(dbg_gate[half * 128:(half + 1) * 128, :],
                                      gate.rearrange("p h a b -> p h (a b)")[:, half, :]
                                      ).then_inc(sDBG, 16)
                        if half == 1:
                            eng.wait_ge(sACC, 2048)
                            eng.dma_start(dbg_acc[128:256, :],
                                          acc.rearrange("p a b -> p (a b)")[:, :]
                                          ).then_inc(sDBG, 16)
                    for tt in range(16):
                        lntail_act_tile(half, tt)

                gexps(0)
                silus(0)
                silus(1)
                lntail_act(1)

            @blk.vector
            def _(eng):
                def gates(half, only_tt=None):
                    for tt in ([only_tt] if only_tt is not None else range(16)):
                        g = 16 * half + tt
                        eng.wait_ge(sRT, g + 1)
                        if g >= 1:
                            eng.wait_ge(sG3, g)
                        if debug:
                            if g >= 1:
                                eng.wait_ge(sDBG2, 16 * g)
                            eng.tensor_copy(gsc[:, 56:64], ps_r[tt % 2][:, 0:NE])
                        eng.reduce_max(gsc[:, 16:17], ps_r[tt % 2][:, 0:NE],
                                       axis=AX.X).then_inc(sG1, 1)
                        eng.wait_ge(sG1, g + 1)
                        def VC(instr):
                            vch_n[0] += 1
                            instr.then_inc(sVCH, 1)
                            eng.wait_ge(sVCH, vch_n[0])
                        VC(eng.tensor_scalar(gsc[:, 0:NE], ps_r[tt % 2][:, 0:NE],
                                             gsc[:, 16:17], None, op0=ALU.subtract))
                        VC(eng.tensor_scalar(gsc[:, 8:8 + NE], gsc[:, 0:NE], 0.0, None,
                                             op0=ALU.is_equal))
                        VC(eng.scalar_tensor_tensor(gsc[:, 32:32 + NE], gsc[:, 8:8 + NE],
                                                    -BIG, gsc[:, 0:NE],
                                                    op0=ALU.mult, op1=ALU.add))
                        eng.reduce_max(gsc[:, 24:25], gsc[:, 32:32 + NE],
                                       axis=AX.X).then_inc(sG2, 1)
                        eng.wait_ge(sG3, g + 1)
                        VC(eng.tensor_scalar(gsc[:, 26:27], gsc[:, 25:26], 1.0, None,
                                             op0=ALU.add))
                        eng.reciprocal(gsc[:, 27:28], gsc[:, 26:27]).then_inc(sG4, 1)
                        eng.wait_ge(sG4, g + 1)
                        VC(eng.tensor_tensor(gsc[:, 28:29], gsc[:, 25:26], gsc[:, 27:28],
                                             op=ALU.mult))
                        VC(eng.tensor_scalar(gsc[:, 40:40 + NE], gsc[:, 32:32 + NE],
                                             gsc[:, 24:25], None, op0=ALU.is_equal))
                        VC(eng.tensor_scalar(gsc[:, 48:48 + NE], gsc[:, 8:8 + NE],
                                             gsc[:, 27:28], None, op0=ALU.mult))
                        eng.scalar_tensor_tensor(gate[:, half, tt, :], gsc[:, 40:40 + NE],
                                                 gsc[:, 28:29], gsc[:, 48:48 + NE],
                                                 op0=ALU.mult, op1=ALU.add
                                                 ).then_inc(sGATE, 1)

                def stts(half):
                    if debug and half == 1:
                        eng.wait_ge(sDBG, 32)
                    eng.memset(acc[:, :, :], 0.0).then_inc(sMS, 1)
                    for i in range(half * 32, (half + 1) * 32):
                        _, e, dq = combo(i)
                        if half == 0 and i == 17:
                            gates(1)
                        for tg in range(4):
                            gidx = i * 4 + tg
                            for dh2 in range(2):
                                ga = gidx * 2 + dh2
                                eng.wait_ge(sYMM, ga + 1)
                                if i % 32 == 0 and tg == 0 and dh2 == 0:
                                    eng.wait_ge(sMS, half + 1)
                                for sub in range(4):
                                    tt = tg * 4 + sub
                                    eng.scalar_tensor_tensor(
                                        acc[:, tt, dh2 * 512:(dh2 + 1) * 512],
                                        ps_y[sub][:, :],
                                        gate[:, half, tt, e:e + 1],
                                        acc[:, tt, dh2 * 512:(dh2 + 1) * 512],
                                        op0=ALU.mult, op1=ALU.add).then_inc(sACC, 1)

                def lntail(half):
                    for tt in range(16):
                        g = half * 16 + tt
                        eng.wait_ge(sXF, 16 * (g + 1))
                        vch_n[0] += 1
                        eng.tensor_tensor(sfin[:, :], xf[:, :], acc[:, tt, :],
                                          op=ALU.add).then_inc(sVCH, 1)
                        eng.wait_ge(sVCH, vch_n[0])
                        eng.nop().then_inc(sFC, 1)
                        eng.reduce_sum(fst[:, 4:5], sfin[:, :], axis=AX.X
                                       ).then_inc(sF1, 1)
                        eng.wait_ge(sF1, g + 1)
                        eng.tensor_scalar(fst[:, 5:6], fst[:, 4:5], 1.0 / E, None,
                                          op0=ALU.mult).then_inc(sF1B, 1)
                        eng.wait_ge(sF1B, g + 1)
                        eng.tensor_scalar(cent[:, :], sfin[:, :], fst[:, 5:6], None,
                                          op0=ALU.subtract).then_inc(sF2, 1)
                        eng.wait_ge(sF4, g + 1)
                        if g >= 1:
                            eng.wait_ge(sOUT, 16 * g)
                        eng.tensor_scalar(outb[:, :], cent[:, :], fst[:, 2:3], None,
                                          op0=ALU.mult).then_inc(sFV, 1)

                gates(0)
                stts(0)
                lntail(0)
                stts(1)
                lntail(1)

        pc.close()
    return nc


def kernel(L0_attn_x, L0_attn_Wq, L0_attn_Wk, L0_attn_Wv, L0_attn_g_scale,
           L0_ffn_W1, L0_ffn_W2, L0_ffn_Wr):
    x = np.ascontiguousarray(np.asarray(L0_attn_x, dtype=np.float32))
    wq = np.asarray(L0_attn_Wq, dtype=np.float32)
    wk = np.asarray(L0_attn_Wk, dtype=np.float32)
    wv = np.asarray(L0_attn_Wv, dtype=np.float32)
    wr = np.asarray(L0_ffn_Wr, dtype=np.float32)
    w1 = np.asarray(L0_ffn_W1, dtype=np.float32)
    w2 = np.asarray(L0_ffn_W2, dtype=np.float32)
    g = np.asarray(L0_attn_g_scale, dtype=np.float32)

    wqT = np.ascontiguousarray(wq.T)
    wkT = np.ascontiguousarray(wk.T)
    wvT = np.ascontiguousarray(wv.T)
    wrT = np.ascontiguousarray(wr.T)
    w1T = np.ascontiguousarray(w1.transpose(0, 2, 1))
    w2T = np.ascontiguousarray(w2.transpose(0, 2, 1))
    g128 = np.ascontiguousarray(np.broadcast_to(g.reshape(1, 1), (128, 1)))
    eps128 = np.ascontiguousarray(np.broadcast_to(np.array([1e-5, 1e-24], dtype=np.float32), (128, 2)))
    ident = np.eye(128, dtype=np.float32)

    nc = build_nc()
    in_maps = []
    for c in range(8):
        in_maps.append({
            "xin": np.ascontiguousarray(x[c]),
            "xtin": np.ascontiguousarray(x[c].T),
            "wqT": wqT, "wkT": wkT, "wvT": wvT, "wrT": wrT,
            "g128": g128, "eps128": eps128, "ident": ident, "w1T": w1T, "w2T": w2T,
        })
    res = run_bass_kernel_spmd(nc, in_maps, core_ids=list(range(8)))
    return np.stack([r["out"] for r in res.results], axis=0)


# version 23
# speedup vs baseline: 1.2167x; 1.0827x over previous
import numpy as np
import contextlib
import concourse.bass as bass
import concourse.mybir as mybir
from concourse.bass_utils import run_bass_kernel_spmd

F32 = mybir.dt.float32
BF16 = mybir.dt.bfloat16
AF = mybir.ActivationFunctionType
ALU = mybir.AluOpType
AX = mybir.AxisListType

S = 4096
E = 1024
H = 16
DH = 64
NT = S // 128
NQT = 8
KC = 32
NE = 8
DFF = 4096
NCOMB = 64
EPS_LN = 1e-5
BIG = 1.0e30


def build_nc(debug=False):
    nc = bass.Bass("TRN2", target_bir_lowering=False)

    x_d = nc.dram_tensor("xin", [S, E], F32, kind="ExternalInput")
    xT_d = nc.dram_tensor("xtin", [E, S], F32, kind="ExternalInput")
    wqT_d = nc.dram_tensor("wqT", [E, E], F32, kind="ExternalInput")
    wkT_d = nc.dram_tensor("wkT", [E, E], F32, kind="ExternalInput")
    wvT_d = nc.dram_tensor("wvT", [E, E], F32, kind="ExternalInput")
    wrT_d = nc.dram_tensor("wrT", [E, NE], F32, kind="ExternalInput")
    g_d = nc.dram_tensor("g128", [128, 1], F32, kind="ExternalInput")
    id_d = nc.dram_tensor("ident", [128, 128], F32, kind="ExternalInput")
    eps_d = nc.dram_tensor("eps128", [128, 2], F32, kind="ExternalInput")
    w1T_d = nc.dram_tensor("w1T", [NE, E, DFF], F32, kind="ExternalInput")
    w2T_d = nc.dram_tensor("w2T", [NE, DFF, E], F32, kind="ExternalInput")
    out_d = nc.dram_tensor("out", [S, E], F32, kind="ExternalOutput")
    dbg = "ExternalOutput" if debug else "Internal"
    scr_qn = nc.dram_tensor("scr_qn", [S, E], BF16, kind=dbg)
    scr_kn = nc.dram_tensor("scr_kn", [S, E], BF16, kind=dbg)
    scr_x1f = nc.dram_tensor("scr_x1f", [S, E], F32, kind=dbg)
    scr_x1b = nc.dram_tensor("scr_x1b", [S, E], BF16, kind=dbg)
    if debug:
        dbg_gate = nc.dram_tensor("dbg_gate", [256, 128], F32, kind="ExternalOutput")
        dbg_gsc = nc.dram_tensor("dbg_gsc", [4096, 64], F32, kind="ExternalOutput")
        dbg_acc = nc.dram_tensor("dbg_acc", [256, 16 * E], BF16, kind="ExternalOutput")

    xT_v = xT_d.rearrange("(c p) n -> p c n", p=128)
    wqT_v = wqT_d.rearrange("(c p) n -> p c n", p=128)
    wkT_v = wkT_d.rearrange("(c p) n -> p c n", p=128)
    wvT_v = wvT_d.rearrange("(c p) n -> p c n", p=128)
    wrT_v = wrT_d.rearrange("(c p) n -> p c n", p=128)
    x_v = x_d.rearrange("(t p) n -> p t n", p=128)
    out_v = out_d.rearrange("(t p) n -> p t n", p=128)
    qn_v = scr_qn.rearrange("(t p) n -> p t n", p=128)
    kn_v = scr_kn.rearrange("(t p) n -> p t n", p=128)
    x1f_v = scr_x1f.rearrange("(t p) n -> p t n", p=128)
    x1b_v = scr_x1b.rearrange("(t p) n -> p t n", p=128)
    w1T_v = w1T_d.rearrange("e (c p) f -> e p c f", p=128)
    w2T_v = w2T_d.rearrange("e (c p) f -> e p c f", p=128)

    ctx = contextlib.ExitStack()
    with ctx:
        g_sb = ctx.enter_context(nc.sbuf_tensor("g_sb", [128, 1], F32))
        eps_sb = ctx.enter_context(nc.sbuf_tensor("eps_sb", [128, 2], F32))
        wr_sb = ctx.enter_context(nc.sbuf_tensor("wr_sb", [128, NE, NE], BF16))
        id_sb = ctx.enter_context(nc.sbuf_tensor("id_sb", [128, 128], BF16))

        def sem(name):
            return ctx.enter_context(nc.semaphore(name))
        sW = sem("sW"); sMMA = sem("sMMA"); sSQ2 = sem("sSQ2"); sCL = sem("sCL")
        sRN = sem("sRN"); sVEA = sem("sVEA"); sSCR = sem("sSCR")
        sTRK = sem("sTRK"); sTRQ = sem("sTRQ"); sXRE = sem("sXRE")
        sSC = sem("sSC"); sEX = sem("sEX"); sAV = sem("sAV"); sQD = sem("sQD")
        sREC = sem("sREC"); sEP = sem("sEP"); sLN1 = sem("sLN1"); sLN2 = sem("sLN2")
        sLN3 = sem("sLN3"); sLN4 = sem("sLN4"); sLNV = sem("sLNV"); sX1W = sem("sX1W")
        sWFF = sem("sWFF"); sWDN = sem("sWDN"); sXT1 = sem("sXT1"); sX1D = sem("sX1D")
        sRT = sem("sRT"); sG1 = sem("sG1"); sG2 = sem("sG2"); sG3 = sem("sG3")
        sG4 = sem("sG4"); sGATE = sem("sGATE"); sMS = sem("sMS")
        sHMM = sem("sHMM"); sSILU = sem("sSILU"); sYMM = sem("sYMM"); sACC = sem("sACC")
        sXF = sem("sXF"); sF1 = sem("sF1"); sF2 = sem("sF2"); sF3 = sem("sF3")
        sF4 = sem("sF4"); sFV = sem("sFV"); sFC = sem("sFC"); sOUT = sem("sOUT")
        sLN1B = sem("sLN1B"); sF1B = sem("sF1B"); sG5 = sem("sG5"); sDBG = sem("sDBG"); sDBG2 = sem("sDBG2")
        sVCH = sem("sVCH"); vch_n = [0]
        sOC = sem("sOC"); sTP = sem("sTP"); sTQ = sem("sTQ"); tq_n = [0]
        sACH = sem("sACH"); ach_n = [0]

        x1T0 = ctx.enter_context(nc.sbuf_tensor("x1T0", [128, 8, 2048], BF16))
        ab = contextlib.ExitStack()
        vaug = ab.enter_context(nc.sbuf_tensor("vaug", [128, KC, H, DH + 1], BF16))
        pa = contextlib.ExitStack()
        wq_sb = pa.enter_context(nc.sbuf_tensor("wq_sb", [128, 8, E], BF16))
        wk_sb = pa.enter_context(nc.sbuf_tensor("wk_sb", [128, 8, E], BF16))
        wv_sb = pa.enter_context(nc.sbuf_tensor("wv_sb", [128, 8, E], BF16))
        xt_sb = pa.enter_context(nc.sbuf_tensor("xt_sb", [128, 4, 8, 128], BF16))
        qn_t = pa.enter_context(nc.sbuf_tensor("qn_t", [128, 2, E], BF16))
        kn_t = pa.enter_context(nc.sbuf_tensor("kn_t", [128, 2, E], BF16))
        ssq = pa.enter_context(nc.sbuf_tensor("ssq", [128, H], F32))
        rn_sb = pa.enter_context(nc.sbuf_tensor("rn_sb", [128, 2 * H], F32))
        sq_scr = pa.enter_context(nc.sbuf_tensor("sq_scr", [128, DH], F32))
        ps_a = [pa.enter_context(nc.psum_tensor(f"ps_a{i}", [128, 2048], F32)) for i in range(2)]

        with nc.Block() as blk:
            @blk.gpsimd
            def _(eng):
                eng.dma_start(g_sb[:, :], g_d[:, :]).then_inc(sW, 16)
                eng.dma_start(eps_sb[:, :], eps_d[:, :]).then_inc(sW, 16)
                eng.dma_start(wq_sb[:, :, :], wqT_v[:, :, :]).then_inc(sW, 16)
                eng.dma_start(wk_sb[:, :, :], wkT_v[:, :, :]).then_inc(sW, 16)
                eng.dma_start(wv_sb[:, :, :], wvT_v[:, :, :]).then_inc(sW, 16)
                eng.dma_start(wr_sb[:, :, :], wrT_v[:, :, :]).then_inc(sW, 16)
                eng.dma_start(id_sb[:, :], id_d[:, :]).then_inc(sW, 16)
                for i in range(2 * NT):
                    t = i % NT
                    if i >= 4:
                        eng.wait_ge(sMMA, i - 3)
                    eng.dma_start(xt_sb[:, i % 4, :, :],
                                  xT_v[:, :, t * 128:(t + 1) * 128]).then_inc(sW, 16)

            @blk.tensor
            def _(eng):
                for i in range(2 * NT):
                    t = i % NT
                    eng.wait_ge(sW, 112 + 16 * (i + 1))
                    if i >= 2:
                        eng.wait_ge(sVEA, i - 1)
                    mm = None
                    kv = i < NT
                    for dc in range(8):
                        lhs = xt_sb[:, i % 4, dc, :]
                        for jj in range(2):
                            if kv:
                                eng.matmul(ps_a[i % 2][:, jj * 512:(jj + 1) * 512], lhs,
                                           wk_sb[:, dc, jj * 512:(jj + 1) * 512],
                                           start=(dc == 0), stop=(dc == 7))
                                mm = eng.matmul(ps_a[i % 2][:, 1024 + jj * 512:1024 + (jj + 1) * 512],
                                                lhs, wv_sb[:, dc, jj * 512:(jj + 1) * 512],
                                                start=(dc == 0), stop=(dc == 7))
                            else:
                                mm = eng.matmul(ps_a[i % 2][:, jj * 512:(jj + 1) * 512], lhs,
                                                wq_sb[:, dc, jj * 512:(jj + 1) * 512],
                                                start=(dc == 0), stop=(dc == 7))
                    mm.then_inc(sMMA, 1)

            @blk.scalar
            def _(eng):
                for i in range(2 * NT):
                    eng.wait_ge(sMMA, i + 1)
                    if i >= 2:
                        eng.wait_ge(sVEA, i - 1)
                    a = None
                    for h in range(H):
                        a = eng.activation(sq_scr[:, :], ps_a[i % 2][:, h * DH:(h + 1) * DH],
                                           AF.Square, accum_out=ssq[:, h:h + 1])
                    a.then_inc(sSQ2, 1)
                    eng.wait_ge(sSQ2, i + 1)
                    ach_n[0] += 1
                    eng.activation(rn_sb[:, (i % 2) * H:(i % 2 + 1) * H], ssq[:, 0:H], AF.Ln,
                                   bias=eps_sb[:, 1:2]).then_inc(sACH, 1)
                    eng.wait_ge(sACH, ach_n[0])
                    eng.activation(rn_sb[:, (i % 2) * H:(i % 2 + 1) * H],
                                   rn_sb[:, (i % 2) * H:(i % 2 + 1) * H], AF.Exp,
                                   scale=-0.5).then_inc(sRN, 1)

            @blk.vector
            def _(eng):
                eng.memset(vaug[:, :, :, :], 1.0)
                for i in range(2 * NT):
                    t = i % NT
                    eng.wait_ge(sRN, i + 1)
                    if i >= 2:
                        eng.wait_ge(sSCR, 16 * (i - 1))
                    kv = i < NT
                    dst = kn_t if kv else qn_t
                    t_ = None
                    for h in range(H):
                        t_ = eng.tensor_scalar(dst[:, i % 2, h * DH:(h + 1) * DH],
                                               ps_a[i % 2][:, h * DH:(h + 1) * DH],
                                               rn_sb[:, (i % 2) * H + h:(i % 2) * H + h + 1],
                                               None, op0=ALU.mult)
                    if kv:
                        eng.tensor_copy(vaug[:, t, :, 0:DH],
                                        ps_a[i % 2][:, 1024:2048].rearrange(
                                            "p (h d) -> p h d", h=H)[:, :, :]
                                        ).then_inc(sVEA, 1)
                    else:
                        t_.then_inc(sVEA, 1)

            @blk.sync
            def _(eng):
                for i in range(2 * NT):
                    t = i % NT
                    eng.wait_ge(sVEA, i + 1)
                    if i < NT:
                        eng.dma_start(kn_v[:, t, :], kn_t[:, i % 2, :]).then_inc(sSCR, 16)
                    else:
                        eng.dma_start(qn_v[:, t, :], qn_t[:, i % 2, :]).then_inc(sSCR, 16)

        pa.close()

        pb = contextlib.ExitStack()
        knT = pb.enter_context(nc.sbuf_tensor("knT", [128, 8, S], BF16))
        qnT = pb.enter_context(nc.sbuf_tensor("qnT", [128, 3, 8, 256], BF16))
        e_sb = pb.enter_context(nc.sbuf_tensor("e_sb", [128, 2, 1024], BF16))
        attn = pb.enter_context(nc.sbuf_tensor("attn", [128, 2, 2, E], F32))
        xres = pb.enter_context(nc.sbuf_tensor("xres", [128, 2, E], F32))
        x1bt = pb.enter_context(nc.sbuf_tensor("x1bt", [128, 2, E], BF16))
        rec = pb.enter_context(nc.sbuf_tensor("rec", [128, 4], F32))
        lnst = pb.enter_context(nc.sbuf_tensor("lnst", [128, 16], F32))
        ps_sp = pb.enter_context(nc.psum_tensor("ps_sp", [128, 2048], F32))
        ps_oa = [pb.enter_context(nc.psum_tensor(f"ps_oa{i}", [128, 512], F32)) for i in range(2)]
        ps_ot = [pb.enter_context(nc.psum_tensor(f"ps_ot{i}", [128, 1024], BF16)) for i in range(2)]
        oa_sb = pb.enter_context(nc.sbuf_tensor("oa_sb", [128, 2, 256], BF16))

        NQ2 = 16

        with nc.Block() as blk:
            @blk.sync
            def _(eng):
                eng.wait_ge(sSCR, 16 * 2 * NT)
                eng.wait_ge(sMMA, 2 * NT)
                for c in range(8):
                    eng.dma_start(knT[:, c, :], scr_kn[:, c * 128:(c + 1) * 128],
                                  transpose=True).then_inc(sTRK, 16)
                def qn_load(qt):
                    for c in range(8):
                        eng.dma_start(qnT[:, qt % 3, c, :],
                                      scr_qn[qt * 256:(qt + 1) * 256, c * 128:(c + 1) * 128],
                                      transpose=True).then_inc(sTRQ, 16)
                qn_load(0)
                qn_load(1)
                for qt in range(NQ2):
                    if qt >= 1:
                        eng.wait_ge(sLNV, qt)
                    eng.dma_start(xres[:, :, :], x_v[:, qt * 2:(qt + 1) * 2, :]
                                  ).then_inc(sXRE, 16)
                    if qt >= 1:
                        eng.dma_start(x1f_v[:, (qt - 1) * 2:qt * 2, :],
                                      attn[:, (qt - 1) % 2, :, :]
                                      ).then_inc(sX1W, 16)
                        eng.dma_start(x1b_v[:, (qt - 1) * 2:qt * 2, :], x1bt[:, :, :]
                                      ).then_inc(sX1W, 16)
                    if qt + 2 < NQ2:
                        eng.wait_ge(sQD, qt)
                        qn_load(qt + 2)
                    if qt == 8:
                        eng.wait_ge(sX1W, 256)
                        for c in range(8):
                            eng.dma_start(x1T0[:, c, :],
                                          scr_x1b[0:2048, c * 128:(c + 1) * 128],
                                          transpose=True).then_inc(sXT1, 16)
                eng.wait_ge(sLNV, NQ2)
                eng.dma_start(x1f_v[:, 30:32, :], attn[:, 1, :, :]).then_inc(sX1W, 16)
                eng.dma_start(x1b_v[:, 30:32, :], x1bt[:, :, :]).then_inc(sX1W, 16)

            @blk.tensor
            def _(eng):
                eng.wait_ge(sTRK, 16 * 8)
                for qt in range(NQ2):
                    eng.wait_ge(sTRQ, 128 * (qt + 1))
                    for hc in range(8):
                        ph = qt * 8 + hc

                        def av_unit(v, last):
                            vn = ph * 16 + v
                            eng.wait_ge(sEX, vn + 1)
                            if v == 0 and ph >= 1:
                                eng.wait_ge(sOC, ph)
                            mm = None
                            for j in range(2):
                                kc = 2 * v + j
                                for hip in range(2):
                                    mm = eng.matmul(
                                        ps_oa[hip][0:DH + 1, 0:256],
                                        vaug[:, kc, 2 * hc + hip, :],
                                        e_sb[:, vn % 2, hip * 512 + j * 256:hip * 512 + (j + 1) * 256],
                                        start=(kc == 0), stop=(last and kc == KC - 1))
                            mm.then_inc(sAV, 1)

                        def otransp(php, sub):
                            if sub == 0:
                                eng.wait_ge(sOC, php + 1)
                                if php >= 1:
                                    eng.wait_ge(sEP, php)
                            else:
                                eng.wait_ge(sTQ, tq_n[0])
                            mm = None
                            for hip in range(2):
                                mm = eng.matmul(
                                    ps_ot[hip][:, sub * 128:sub * 128 + 65],
                                    oa_sb[0:DH + 1, hip, sub * 128:(sub + 1) * 128],
                                    id_sb[0:DH + 1, 0:DH + 1],
                                    start=True, stop=True, is_transpose=True)
                            if sub == 0:
                                tq_n[0] += 1
                                mm.then_inc(sTQ, 1)
                            else:
                                mm.then_inc(sTP, 1)

                        for u in range(16):
                            un = ph * 16 + u
                            if un >= 2:
                                eng.wait_ge(sEX, un - 1)
                            base = (u % 2) * 1024
                            for j in range(2):
                                kc = 2 * u + j
                                for hip in range(2):
                                    col = base + hip * 512 + j * 256
                                    mm = eng.matmul(
                                        ps_sp[:, col:col + 256],
                                        knT[hip * DH:(hip + 1) * DH, hc, kc * 128:(kc + 1) * 128],
                                        qnT[hip * DH:(hip + 1) * DH, qt % 3, hc, :],
                                        start=True, stop=True)
                            mm.then_inc(sSC, 1)
                            if u == 1 and ph >= 1:
                                otransp(ph - 1, 0)
                                otransp(ph - 1, 1)
                            if u >= 1:
                                av_unit(u - 1, last=False)
                        av_unit(15, last=True)
                        if ph == NQ2 * 8 - 1:
                            otransp(ph, 0)
                            otransp(ph, 1)
                    eng.nop().then_inc(sQD, 1)

            @blk.scalar
            def _(eng):
                def ln_act(qt):
                    eng.wait_ge(sLN2, qt + 1)
                    if qt >= 1:
                        eng.wait_ge(sX1W, 32 * qt)
                    a = None
                    for sub in range(2):
                        a = eng.activation(x1bt[:, 0, :], xres[:, sub, :], AF.Square,
                                           accum_out=lnst[:, 8 + sub:9 + sub])
                    a.then_inc(sLN3, 1)
                    eng.wait_ge(sLN3, qt + 1)
                    ach_n[0] += 1
                    eng.activation(lnst[:, 12:14], lnst[:, 8:10], AF.Ln,
                                   scale=1.0 / E, bias=eps_sb[:, 0:1]).then_inc(sACH, 1)
                    eng.wait_ge(sACH, ach_n[0])
                    eng.activation(lnst[:, 12:14], lnst[:, 12:14], AF.Exp,
                                   scale=-0.5).then_inc(sLN4, 1)

                eng.wait_ge(sW, 16)
                for qt in range(NQ2):
                    for hc in range(8):
                        ph = qt * 8 + hc
                        for u in range(16):
                            un = ph * 16 + u
                            eng.wait_ge(sSC, un + 1)
                            if un >= 2:
                                eng.wait_ge(sAV, un - 1)
                            eng.activation(e_sb[:, un % 2, :],
                                           ps_sp[:, (u % 2) * 1024:(u % 2 + 1) * 1024],
                                           AF.Exp, scale=g_sb[:, 0:1]).then_inc(sEX, 1)
                            if hc == 0 and u == 3 and qt >= 1:
                                ln_act(qt - 1)
                ln_act(NQ2 - 1)

            @blk.vector
            def _(eng):
                def ln_v1(qt):
                    eng.wait_ge(sXRE, 16 * (qt + 1))
                    vch_n[0] += 1
                    eng.tensor_tensor(attn[:, qt % 2, :, :], attn[:, qt % 2, :, :],
                                      xres[:, :, :],
                                      op=ALU.add).then_inc(sVCH, 1)
                    eng.wait_ge(sVCH, vch_n[0])
                    eng.reduce_sum(lnst[:, 0:2].rearrange("p (c n) -> p c n", c=2)[:, :, :],
                                   attn[:, qt % 2, :, :], axis=AX.X).then_inc(sLN1, 1)
                    eng.wait_ge(sLN1, qt + 1)
                    eng.tensor_scalar(lnst[:, 4:6], lnst[:, 0:2], 1.0 / E, None,
                                      op0=ALU.mult).then_inc(sLN1B, 1)
                    eng.wait_ge(sLN1B, qt + 1)
                    if qt >= 1:
                        eng.wait_ge(sLN3, qt)
                    t_ = None
                    for sub in range(2):
                        t_ = eng.tensor_scalar(xres[:, sub, :], attn[:, qt % 2, sub, :],
                                               lnst[:, 4 + sub:5 + sub], None,
                                               op0=ALU.subtract)
                    t_.then_inc(sLN2, 1)

                def ln_v2(qt):
                    eng.wait_ge(sLN4, qt + 1)
                    if qt >= 1:
                        eng.wait_ge(sX1W, 32 * qt)
                    t_ = None
                    for sub in range(2):
                        t_ = eng.tensor_scalar(attn[:, qt % 2, sub, :], xres[:, sub, :],
                                               lnst[:, 12 + sub:13 + sub], None, op0=ALU.mult)
                    vch_n[0] += 1
                    t_.then_inc(sVCH, 1)
                    eng.wait_ge(sVCH, vch_n[0])
                    eng.tensor_copy(x1bt[:, :, :], attn[:, qt % 2, :, :]).then_inc(sLNV, 1)

                for qt in range(NQ2):
                    for hc in range(8):
                        ph = qt * 8 + hc
                        eng.wait_ge(sAV, 16 * (ph + 1))
                        if ph >= 1:
                            eng.wait_ge(sTP, ph)
                        c_ = None
                        for hip in range(2):
                            c_ = eng.tensor_copy(oa_sb[0:DH + 1, hip, :],
                                                 ps_oa[hip][0:DH + 1, 0:256])
                        c_.then_inc(sOC, 1)
                        eng.wait_ge(sTP, ph + 1)
                        r = None
                        for hip in range(2):
                            for sub in range(2):
                                i = hip * 2 + sub
                                r = eng.reciprocal(rec[:, i:i + 1],
                                                   ps_ot[hip][:, sub * 128 + DH:sub * 128 + DH + 1])
                        r.then_inc(sREC, 1)
                        eng.wait_ge(sREC, ph + 1)
                        if hc == 0 and qt >= 2:
                            eng.wait_ge(sX1W, 32 * (qt - 1))
                        t_ = None
                        for hip in range(2):
                            for sub in range(2):
                                i = hip * 2 + sub
                                t_ = eng.tensor_scalar(
                                    attn[:, qt % 2, sub, (2 * hc + hip) * DH:(2 * hc + hip + 1) * DH],
                                    ps_ot[hip][:, sub * 128:sub * 128 + DH], rec[:, i:i + 1],
                                    None, op0=ALU.mult)
                        t_.then_inc(sEP, 1)
                        if hc == 0 and qt >= 1:
                            ln_v2(qt - 1)
                    ln_v1(qt)
                ln_v2(NQ2 - 1)

        pb.close()
        ab.close()


        pc = contextlib.ExitStack()
        w1s = [pc.enter_context(nc.sbuf_tensor(f"w1s{k}", [128, 8, 1024], BF16)) for k in range(2)]
        w2s = [pc.enter_context(nc.sbuf_tensor(f"w2s{k}", [128, 8, 1024], BF16)) for k in range(2)]
        x1T1 = pc.enter_context(nc.sbuf_tensor("x1T1", [128, 8, 2048], BF16))
        acc = pc.enter_context(nc.sbuf_tensor("acc", [128, 16, E], BF16))
        hT = pc.enter_context(nc.sbuf_tensor("hT", [128, 8, 512], BF16))
        gate = pc.enter_context(nc.sbuf_tensor("gate", [128, 2, 16, NE], F32))
        gsc = pc.enter_context(nc.sbuf_tensor("gsc", [128, 64], F32))
        xf = pc.enter_context(nc.sbuf_tensor("xf", [128, 2, E], F32))
        sfin = pc.enter_context(nc.sbuf_tensor("sfin", [128, 2, E], F32))
        cent = pc.enter_context(nc.sbuf_tensor("cent", [128, 2, E], F32))
        outb = pc.enter_context(nc.sbuf_tensor("outb", [128, 2, E], F32))
        sq_c = pc.enter_context(nc.sbuf_tensor("sq_c", [128, E], F32))
        fst = pc.enter_context(nc.sbuf_tensor("fst", [128, 16], F32))
        ps_h = [pc.enter_context(nc.psum_tensor(f"ps_h{i}", [128, 512], F32)) for i in range(2)]
        ps_y = [pc.enter_context(nc.psum_tensor(f"ps_y{i}", [128, 512], F32)) for i in range(4)]
        ps_r = [pc.enter_context(nc.psum_tensor(f"ps_r{i}", [128, 512], F32)) for i in range(2)]

        def combo(i):
            return i // 32, (i % 32) // 4, (i % 32) % 4

        with nc.Block() as blk:
            @blk.gpsimd
            def _(eng):
                for i in range(NCOMB):
                    half, e, dq = combo(i)
                    if i == 0:
                        eng.wait_ge(sX1W, 512)
                    if i >= 2:
                        eng.wait_ge(sWDN, i - 1)
                    k = i % 2
                    eng.dma_start(w1s[k][:, :, :],
                                  w1T_v[e, :, :, dq * 1024:(dq + 1) * 1024]
                                  ).then_inc(sWFF, 16)
                    eng.dma_start(w2s[k][:, :, :],
                                  w2T_v[e, :, dq * 8:(dq + 1) * 8, :]
                                  ).then_inc(sWFF, 16)

            @blk.sync
            def _(eng):
                eng.wait_ge(sX1W, 512)
                for c in range(8):
                    eng.dma_start(x1T1[:, c, :],
                                  scr_x1b[2048:4096, c * 128:(c + 1) * 128],
                                  transpose=True).then_inc(sXT1, 16)
                for half in range(2):
                    for tt in range(16):
                        g = half * 16 + tt
                        eng.wait_ge(sACC, 1024 * (half + 1))
                        if g >= 2:
                            eng.wait_ge(sFC, g - 1)
                        eng.dma_start(xf[:, g % 2, :], x1f_v[:, g, :]).then_inc(sXF, 16)
                        if tt >= 1:
                            eng.wait_ge(sFV, g)
                            eng.dma_start(out_v[:, g - 1, :], outb[:, (g - 1) % 2, :]
                                          ).then_inc(sOUT, 16)
                    eng.wait_ge(sFV, (half + 1) * 16)
                    eng.dma_start(out_v[:, half * 16 + 15, :], outb[:, (half * 16 + 15) % 2, :]
                                  ).then_inc(sOUT, 16)
                eng.wait_ge(sOUT, 16 * 32)

            @blk.tensor
            def _(eng):
                def router(half, only_tt=None):
                    for tt in ([only_tt] if only_tt is not None else range(16)):
                        if tt == 0:
                            eng.wait_ge(sXT1, 128 * (half + 1))
                            eng.wait_ge(sW, 112)
                            if half == 0:
                                eng.wait_ge(sLNV, 16)
                                eng.wait_ge(sEX, 2048)
                                eng.wait_ge(sEP, 128)
                        if tt >= 2:
                            eng.wait_ge(sGATE, 16 * half + tt - 1)
                        mm = None
                        for dc in range(8):
                            mm = eng.matmul(ps_r[tt % 2][:, 0:NE],
                                            (x1T0 if half == 0 else x1T1)[:, dc, tt * 128:(tt + 1) * 128],
                                            wr_sb[:, dc, :],
                                            start=(dc == 0), stop=(dc == 7))
                        mm.then_inc(sRT, 1)
                for half in range(2):
                    if half == 0:
                        router(0)
                    for i in range(half * 32, (half + 1) * 32):
                        if half == 0 and i == 17:
                            router(1)
                        _, e, dq = combo(i)
                        k = i % 2
                        eng.wait_ge(sWFF, 32 * (i + 1))
                        for tg in range(4):
                            gidx = i * 4 + tg
                            for c in range(8):
                                gs = gidx * 8 + c
                                if gs >= 2:
                                    eng.wait_ge(sSILU, gs - 1)
                                mm = None
                                for dc in range(8):
                                    mm = eng.matmul(ps_h[c % 2][:, :],
                                                    w1s[k][:, dc, c * 128:(c + 1) * 128],
                                                    (x1T0 if half == 0 else x1T1)[:, dc, tg * 512:(tg + 1) * 512],
                                                    start=(dc == 0), stop=(dc == 7))
                                mm.then_inc(sHMM, 1)
                            for dh2 in range(2):
                                ga = gidx * 2 + dh2
                                mm = None
                                for sub in range(4):
                                    if ga >= 1:
                                        eng.wait_ge(sACC, 4 * (ga - 1) + sub + 1)
                                    for c in range(8):
                                        if dh2 == 0 and sub == 0:
                                            eng.wait_ge(sSILU, 8 * gidx + c + 1)
                                        mm = eng.matmul(ps_y[sub][:, :],
                                                        hT[:, c, sub * 128:(sub + 1) * 128],
                                                        w2s[k][:, c, dh2 * 512:(dh2 + 1) * 512],
                                                        start=(c == 0), stop=(c == 7))
                                mm.then_inc(sYMM, 1)
                        eng.nop().then_inc(sWDN, 1)

            @blk.scalar
            def _(eng):
                def gexps(half, only_tt=None):
                    for tt in ([only_tt] if only_tt is not None else range(16)):
                        g = 16 * half + tt
                        eng.wait_ge(sG2, g + 1)
                        eng.activation(gsc[:, 25:26], gsc[:, 24:25], AF.Exp
                                       ).then_inc(sG3, 1)
                        if debug:
                            eng.wait_ge(sGATE, g + 1)
                            eng.dma_start(dbg_gsc[g * 128:(g + 1) * 128, :],
                                          gsc[:, :]).then_inc(sDBG2, 16)

                def silus(half):
                    if debug and half == 1:
                        eng.wait_ge(sACC, 1024)
                        eng.dma_start(dbg_acc[0:128, :],
                                      acc.rearrange("p a b -> p (a b)")[:, :]
                                      ).then_inc(sDBG, 16)
                    for i in range(half * 32, (half + 1) * 32):
                        if half == 0 and i == 17:
                            gexps(1)
                        if half == 1 and i == 32:
                            for _tt in range(16):
                                lntail_act_tile(0, _tt)
                        for tg in range(4):
                            gidx = i * 4 + tg
                            for c in range(8):
                                gs = gidx * 8 + c
                                eng.wait_ge(sHMM, gs + 1)
                                if c == 0 and gidx >= 1:
                                    eng.wait_ge(sYMM, 2 * gidx)
                                eng.activation(hT[:, c, :], ps_h[c % 2][:, :],
                                               AF.Silu).then_inc(sSILU, 1)

                def lntail_act_tile(half, tt):
                    g = half * 16 + tt
                    p = g % 2
                    eng.wait_ge(sF2, g + 1)
                    eng.activation(sq_c[:, :], cent[:, p, :], AF.Square,
                                   accum_out=fst[:, 8 * p:8 * p + 1]).then_inc(sF3, 1)
                    eng.wait_ge(sF3, g + 1)
                    ach_n[0] += 1
                    eng.activation(fst[:, 8 * p + 1:8 * p + 2], fst[:, 8 * p:8 * p + 1], AF.Ln,
                                   scale=1.0 / E, bias=eps_sb[:, 0:1]).then_inc(sACH, 1)
                    eng.wait_ge(sACH, ach_n[0])
                    eng.activation(fst[:, 8 * p + 2:8 * p + 3], fst[:, 8 * p + 1:8 * p + 2], AF.Exp,
                                   scale=-0.5).then_inc(sF4, 1)

                def lntail_act(half):
                    if debug:
                        eng.wait_ge(sGATE, 16 * (half + 1))
                        eng.dma_start(dbg_gate[half * 128:(half + 1) * 128, :],
                                      gate.rearrange("p h a b -> p h (a b)")[:, half, :]
                                      ).then_inc(sDBG, 16)
                        if half == 1:
                            eng.wait_ge(sACC, 2048)
                            eng.dma_start(dbg_acc[128:256, :],
                                          acc.rearrange("p a b -> p (a b)")[:, :]
                                          ).then_inc(sDBG, 16)
                    for tt in range(16):
                        lntail_act_tile(half, tt)

                gexps(0)
                silus(0)
                silus(1)
                lntail_act(1)

            @blk.vector
            def _(eng):
                def gates(half, only_tt=None):
                    for tt in ([only_tt] if only_tt is not None else range(16)):
                        g = 16 * half + tt
                        eng.wait_ge(sRT, g + 1)
                        if g >= 1:
                            eng.wait_ge(sG3, g)
                        if debug:
                            if g >= 1:
                                eng.wait_ge(sDBG2, 16 * g)
                            eng.tensor_copy(gsc[:, 56:64], ps_r[tt % 2][:, 0:NE])
                        eng.reduce_max(gsc[:, 16:17], ps_r[tt % 2][:, 0:NE],
                                       axis=AX.X).then_inc(sG1, 1)
                        eng.wait_ge(sG1, g + 1)
                        def VC(instr):
                            vch_n[0] += 1
                            instr.then_inc(sVCH, 1)
                            eng.wait_ge(sVCH, vch_n[0])
                        VC(eng.tensor_scalar(gsc[:, 0:NE], ps_r[tt % 2][:, 0:NE],
                                             gsc[:, 16:17], None, op0=ALU.subtract))
                        VC(eng.tensor_scalar(gsc[:, 8:8 + NE], gsc[:, 0:NE], 0.0, None,
                                             op0=ALU.is_equal))
                        VC(eng.scalar_tensor_tensor(gsc[:, 32:32 + NE], gsc[:, 8:8 + NE],
                                                    -BIG, gsc[:, 0:NE],
                                                    op0=ALU.mult, op1=ALU.add))
                        eng.reduce_max(gsc[:, 24:25], gsc[:, 32:32 + NE],
                                       axis=AX.X).then_inc(sG2, 1)
                        eng.wait_ge(sG3, g + 1)
                        VC(eng.tensor_scalar(gsc[:, 26:27], gsc[:, 25:26], 1.0, None,
                                             op0=ALU.add))
                        eng.reciprocal(gsc[:, 27:28], gsc[:, 26:27]).then_inc(sG4, 1)
                        eng.wait_ge(sG4, g + 1)
                        VC(eng.tensor_tensor(gsc[:, 28:29], gsc[:, 25:26], gsc[:, 27:28],
                                             op=ALU.mult))
                        VC(eng.tensor_scalar(gsc[:, 40:40 + NE], gsc[:, 32:32 + NE],
                                             gsc[:, 24:25], None, op0=ALU.is_equal))
                        VC(eng.tensor_scalar(gsc[:, 48:48 + NE], gsc[:, 8:8 + NE],
                                             gsc[:, 27:28], None, op0=ALU.mult))
                        eng.scalar_tensor_tensor(gate[:, half, tt, :], gsc[:, 40:40 + NE],
                                                 gsc[:, 28:29], gsc[:, 48:48 + NE],
                                                 op0=ALU.mult, op1=ALU.add
                                                 ).then_inc(sGATE, 1)

                def stts(half):
                    if debug and half == 1:
                        eng.wait_ge(sDBG, 32)
                    eng.memset(acc[:, :, :], 0.0).then_inc(sMS, 1)
                    for i in range(half * 32, (half + 1) * 32):
                        _, e, dq = combo(i)
                        if half == 0 and i == 17:
                            gates(1)
                        for tg in range(4):
                            gidx = i * 4 + tg
                            for dh2 in range(2):
                                ga = gidx * 2 + dh2
                                eng.wait_ge(sYMM, ga + 1)
                                if i % 32 == 0 and tg == 0 and dh2 == 0:
                                    eng.wait_ge(sMS, half + 1)
                                for sub in range(4):
                                    tt = tg * 4 + sub
                                    eng.scalar_tensor_tensor(
                                        acc[:, tt, dh2 * 512:(dh2 + 1) * 512],
                                        ps_y[sub][:, :],
                                        gate[:, half, tt, e:e + 1],
                                        acc[:, tt, dh2 * 512:(dh2 + 1) * 512],
                                        op0=ALU.mult, op1=ALU.add).then_inc(sACC, 1)

                def lntail(half):
                    for tt in range(16):
                        g = half * 16 + tt
                        p = g % 2
                        eng.wait_ge(sXF, 16 * (g + 1))
                        vch_n[0] += 1
                        eng.tensor_tensor(sfin[:, p, :], xf[:, p, :], acc[:, tt, :],
                                          op=ALU.add).then_inc(sVCH, 1)
                        eng.wait_ge(sVCH, vch_n[0])
                        eng.nop().then_inc(sFC, 1)
                        eng.reduce_sum(fst[:, 8 * p + 4:8 * p + 5], sfin[:, p, :], axis=AX.X
                                       ).then_inc(sF1, 1)
                        eng.wait_ge(sF1, g + 1)
                        eng.tensor_scalar(fst[:, 8 * p + 5:8 * p + 6], fst[:, 8 * p + 4:8 * p + 5],
                                          1.0 / E, None,
                                          op0=ALU.mult).then_inc(sF1B, 1)
                        eng.wait_ge(sF1B, g + 1)
                        eng.tensor_scalar(cent[:, p, :], sfin[:, p, :],
                                          fst[:, 8 * p + 5:8 * p + 6], None,
                                          op0=ALU.subtract).then_inc(sF2, 1)
                        eng.wait_ge(sF4, g + 1)
                        if g >= 2:
                            eng.wait_ge(sOUT, 16 * (g - 1))
                        eng.tensor_scalar(outb[:, p, :], cent[:, p, :],
                                          fst[:, 8 * p + 2:8 * p + 3], None,
                                          op0=ALU.mult).then_inc(sFV, 1)

                gates(0)
                stts(0)
                lntail(0)
                stts(1)
                lntail(1)

        pc.close()
    return nc


def kernel(L0_attn_x, L0_attn_Wq, L0_attn_Wk, L0_attn_Wv, L0_attn_g_scale,
           L0_ffn_W1, L0_ffn_W2, L0_ffn_Wr):
    x = np.ascontiguousarray(np.asarray(L0_attn_x, dtype=np.float32))
    wq = np.asarray(L0_attn_Wq, dtype=np.float32)
    wk = np.asarray(L0_attn_Wk, dtype=np.float32)
    wv = np.asarray(L0_attn_Wv, dtype=np.float32)
    wr = np.asarray(L0_ffn_Wr, dtype=np.float32)
    w1 = np.asarray(L0_ffn_W1, dtype=np.float32)
    w2 = np.asarray(L0_ffn_W2, dtype=np.float32)
    g = np.asarray(L0_attn_g_scale, dtype=np.float32)

    wqT = np.ascontiguousarray(wq.T)
    wkT = np.ascontiguousarray(wk.T)
    wvT = np.ascontiguousarray(wv.T)
    wrT = np.ascontiguousarray(wr.T)
    w1T = np.ascontiguousarray(w1.transpose(0, 2, 1))
    w2T = np.ascontiguousarray(w2.transpose(0, 2, 1))
    g128 = np.ascontiguousarray(np.broadcast_to(g.reshape(1, 1), (128, 1)))
    eps128 = np.ascontiguousarray(np.broadcast_to(np.array([1e-5, 1e-24], dtype=np.float32), (128, 2)))
    ident = np.eye(128, dtype=np.float32)

    nc = build_nc()
    in_maps = []
    for c in range(8):
        in_maps.append({
            "xin": np.ascontiguousarray(x[c]),
            "xtin": np.ascontiguousarray(x[c].T),
            "wqT": wqT, "wkT": wkT, "wvT": wvT, "wrT": wrT,
            "g128": g128, "eps128": eps128, "ident": ident, "w1T": w1T, "w2T": w2T,
        })
    res = run_bass_kernel_spmd(nc, in_maps, core_ids=list(range(8)))
    return np.stack([r["out"] for r in res.results], axis=0)
